# version 11
# baseline (speedup 1.0000x reference)
import numpy as np
import concourse.bass as bass
import concourse.mybir as mybir
from concourse.bass_utils import run_bass_kernel_spmd

F32 = mybir.dt.float32
BF16 = mybir.dt.bfloat16
AF = mybir.ActivationFunctionType
ALU = mybir.AluOpType
AX = mybir.AxisListType

N_CORES = 8
T = 4096
D = 1024
NQ = 4
TQ = T // NQ
NT = TQ // 128
E = 16
IM = 512
NH = 16
DH = 64

import os as _os
ATT_LVL = int(_os.environ.get("K_ATT", "9"))
DO_ATT = ATT_LVL > 0
DO_SH = _os.environ.get("K_SH", "1") == "1"
DO_MOE = _os.environ.get("K_MOE", "1") == "1"

NDQ = 4
NGQ = 6
SYNC_QS = [f"sync{i}" for i in range(NDQ)]
GPS_QS = [f"gpsimd{i}" for i in range(NGQ)]
ENGINES = ["tensor", "vector", "scalar"] + SYNC_QS + GPS_QS
DMA_ENGINES = set(SYNC_QS) | set(GPS_QS)
REAL_ENGINE = {e: e for e in ["tensor", "vector", "scalar"]}
REAL_ENGINE.update({q: "sync" for q in SYNC_QS})
REAL_ENGINE.update({q: "gpsimd" for q in GPS_QS})


class Op:
    __slots__ = ("eng", "fn", "deps", "inc_step", "need_inc", "val", "idx")

    def __init__(self, eng, fn, idx):
        self.eng = eng
        self.fn = fn
        self.idx = idx
        self.deps = set()
        self.inc_step = 16 if eng in DMA_ENGINES else 1
        self.need_inc = False
        self.val = None


class Res:
    __slots__ = ("name", "last_write", "readers")

    def __init__(self, name):
        self.name = name
        self.last_write = None
        self.readers = []


class Prog:
    def __init__(self):
        self.ops = []
        self.rr = {"sync": 0, "gpsimd": 0}
        self.qlast = {q: None for q in DMA_ENGINES}

    def emit(self, eng, fn, reads=(), writes=()):
        if eng in ("sync", "gpsimd"):
            qi = self.rr[eng]
            self.rr[eng] = (qi + 1) % (NDQ if eng == "sync" else NGQ)
            eng = f"{eng}{qi}"
        op = Op(eng, fn, len(self.ops))
        for r in reads:
            if r.last_write is not None:
                op.deps.add(r.last_write)
        for w in writes:
            if w.last_write is not None:
                op.deps.add(w.last_write)
            for rd in w.readers:
                op.deps.add(rd)
        if eng in DMA_ENGINES:
            if self.qlast[eng] is not None:
                op.deps.add(self.qlast[eng])
            self.qlast[eng] = op
        op.deps = {d for d in op.deps if d.eng != eng or eng != "tensor"}
        for w in writes:
            w.last_write = op
            w.readers = []
        for r in reads:
            r.readers.append(op)
        self.ops.append(op)
        return op

    def finalize(self, block, sems):
        for op in self.ops:
            if op.eng in DMA_ENGINES:
                op.need_inc = True
            for d in op.deps:
                d.need_inc = True
        counters = {e: 0 for e in ENGINES}
        for op in self.ops:
            if op.need_inc:
                counters[op.eng] += op.inc_step
                op.val = counters[op.eng]
        for real in ["tensor", "vector", "scalar", "sync", "gpsimd"]:
            eng_ops = [o for o in self.ops if REAL_ENGINE[o.eng] == real]

            def body(eng, eng_ops=eng_ops):
                waited = {}
                for op in eng_ops:
                    for d in sorted(op.deps, key=lambda o: o.idx):
                        if d.val is None:
                            continue
                        if waited.get(d.eng, -1) >= d.val:
                            continue
                        eng.wait_ge(sems[d.eng], d.val)
                        waited[d.eng] = d.val
                    inst = op.fn(eng)
                    if op.need_inc:
                        inst.then_inc(sems[op.eng], op.inc_step)

            getattr(block, real)(body)


def build_program():
    nc = bass.Bass("TRN2", target_bir_lowering=False)

    def din(name, shape):
        return nc.dram_tensor(name, shape, F32, kind="ExternalInput").ap()

    x = din("x", [T, D])
    xTt = din("xTt", [T // 128, D, 128])
    wqkv = din("wqkv", [D, 3 * D])
    wout = din("wout", [D, D])
    rwT = din("rwT", [D, E])
    ebias_rep = din("ebias_rep", [1, E * NT])
    eps16_rep = din("eps16_rep", [1, E * NT])
    epsg_rep = din("epsg_rep", [1, 4 * NT])
    gwT = din("gwT", [E, D, IM])
    uwT = din("uwT", [E, D, IM])
    dwT = din("dwT", [E, IM, D])
    shgT = din("shgT", [D, D])
    shuT = din("shuT", [D, D])
    shdT = din("shdT", [D, D])
    identf = din("identf", [128, 128])
    out = nc.dram_tensor("out", [T, D], F32, kind="ExternalOutput").ap()
    DBG = _os.environ.get("K_DBG", "0") == "1"
    if DBG:
        dbg_xnT = nc.dram_tensor("dbg_xnT", [128, 8 * TQ], BF16,
                                 kind="ExternalOutput").ap()
        dbg_r = nc.dram_tensor("dbg_r", [128, 3 * E * NT], F32,
                               kind="ExternalOutput").ap()

    def sb(name, shape, dt=F32):
        return nc.alloc_sbuf_tensor(name, shape, dt).ap()

    acc = sb("acc", [128, NT * D])
    xnT = sb("xnT", [128, 8 * TQ], BF16)
    warea = sb("warea", [128, 6 * 4096], BF16)
    big4 = sb("big4", [128, 16 * TQ], BF16)
    aoT = sb("aoT", [128, 8 * TQ], BF16)
    v_sb = sb("v_sb", [128, 4 * TQ], BF16)
    hbuf = sb("hbuf", [128, 4 * TQ], BF16)
    xn_nat = sb("xn_nat", [128, 2 * D], BF16)
    xt_sb = sb("xt_sb", [128, 2 * D])
    exp_sb = sb("exp_sb", [128, 2 * 512])
    attn_sb = sb("attn_sb", [128, 2 * 512], BF16)
    attnT_sb = sb("attnT_sb", [128, 2 * 512], BF16)
    sil_sb = sb("sil_sb", [128, 2 * 512], BF16)
    ub_sb = sb("ub_sb", [128, 2 * 512], BF16)
    stbuf = sb("stbuf", [128, 12 * NT])
    lnscr = sb("lnscr", [128, 96])
    zbuf = sb("zbuf", [128, E * NT])
    scores = sb("scores", [128, E * NT])
    r1 = sb("r1", [128, E * NT])
    r2 = sb("r2", [128, E * NT])
    r3 = sb("r3", [128, E * NT])
    comb = sb("comb", [128, 2 * E * NT])
    gsb = sb("gsb", [128, 4 * NT])
    gscr = sb("gscr", [128, 6 * NT])
    m2b = sb("m2b", [128, NT])
    out8 = sb("out8", [128, 8])
    rsb = sb("rsb", [128, NT])
    rrec = sb("rrec", [128, NT])
    rt16 = sb("rt16", [128, E])
    sm8 = sb("sm8", [128, 8 * NT])
    rwT_sb = sb("rwT_sb", [128, 8 * E])
    rwsum_b = sb("rwsum_b", [128, E])
    ebias_b = sb("ebias_b", [128, E * NT])
    eps16_b = sb("eps16_b", [128, E * NT])
    epsg_b = sb("epsg_b", [128, 4 * NT])
    identb = sb("identb", [128, 128], BF16)
    ones_r = sb("ones_r", [1, 128])
    ones_c = sb("ones_c", [128, 1])
    row_st = sb("row_st", [1, 4 * E * NT])

    PB = nc.alloc_psum_tensor("PB", [128, 1024], BF16).ap()
    PAV = nc.alloc_psum_tensor("PAV", [128, 512], F32).ap()
    PG = nc.alloc_psum_tensor("PG", [128, 512], F32).ap()
    PU = nc.alloc_psum_tensor("PU", [128, 512], F32).ap()
    PD0 = nc.alloc_psum_tensor("PD0", [128, 512], F32).ap()
    PD1 = nc.alloc_psum_tensor("PD1", [128, 512], F32).ap()
    PS = nc.alloc_psum_tensor("PS", [128, 512], F32).ap()
    PR = nc.alloc_psum_tensor("PR", [128, 512], F32).ap()

    p = Prog()

    r_acc = [Res(f"acc{t}") for t in range(NT)]
    r_xnT = Res("xnT")
    r_wblk = [Res(f"wblk{i}") for i in range(6)]
    r_big = [Res(f"big{i}") for i in range(16)]
    r_aoT = [Res(f"aoT{i}") for i in range(8)]
    r_vsb = [Res(f"vsb{i}") for i in range(4)]
    r_hb = [Res(f"hb{i}") for i in range(4)]
    r_xnn = [Res(f"xnn{i}") for i in range(2)]
    r_xts = [Res(f"xts{i}") for i in range(2)]
    r_exps = [Res(f"exps{i}") for i in range(4)]
    r_attns = [Res(f"attns{i}") for i in range(4)]
    r_attnT = [Res(f"attnT{i}") for i in range(4)]
    r_sil = [Res(f"sil{i}") for i in range(2)]
    r_ub = [Res(f"ub{i}") for i in range(2)]
    r_st = Res("stbuf")
    r_ln2 = [Res("lnscr0"), Res("lnscr1")]
    r_z = Res("zbuf")
    r_scores = Res("scores")
    r_route = Res("route_scratch")
    r_comb2 = [Res("comb0"), Res("comb1")]
    r_rt16 = Res("rt16")
    r_sm8 = [Res(f"sm8{i}") for i in range(4)]
    r_out = Res("out_hbm")
    r_pb = {id(PB): Res("PB"), id(PAV): Res("PAV"), id(PG): Res("PG"),
            id(PU): Res("PU"), id(PD0): Res("PD0"), id(PD1): Res("PD1"),
            id(PS): Res("PS"), id(PR): Res("PR")}
    r_const = Res("consts")

    def RP(bank):
        return r_pb[id(bank)]

    def mm_group(bank, lhsrhs, reads, out_ap=None, tile_position=None):
        ob = bank if out_ap is None else out_ap
        n = len(lhsrhs)
        for i, (lh, rh) in enumerate(lhsrhs):
            p.emit("tensor",
                   lambda e, lh=lh, rh=rh, i=i, n=n, ob=ob: e.matmul(
                       ob, lh, rh, start=(i == 0), stop=(i == n - 1),
                       tile_position=tile_position),
                   reads=reads, writes=(RP(bank),))

    p.emit("sync", lambda e: e.dma_start(
        out=rwT_sb.rearrange("p (k e) -> p k e", k=8),
        in_=rwT.rearrange("(k p) e -> p k e", p=128)),
        writes=(r_const,))
    p.emit("gpsimd", lambda e: e.dma_start(out=identb, in_=identf),
           writes=(r_const,))
    p.emit("sync", lambda e: e.dma_start(
        out=row_st[:, 0:128], in_=ebias_rep), writes=(r_const,))
    p.emit("sync", lambda e: e.dma_start(
        out=row_st[:, 128:256], in_=eps16_rep), writes=(r_const,))
    p.emit("sync", lambda e: e.dma_start(
        out=row_st[:, 256:288], in_=epsg_rep), writes=(r_const,))
    p.emit("vector", lambda e: e.memset(ones_r, 1.0), writes=(r_const,))
    p.emit("vector", lambda e: e.memset(ones_c, 1.0), writes=(r_const,))
    mm_group(PR, [(ones_c, rwT_sb[:, 16 * k:16 * k + 16]) for k in range(8)],
             reads=(r_const,), out_ap=PR[0:1, 0:16])
    p.emit("vector", lambda e: e.tensor_copy(row_st[:, 288:304], PR[0:1, 0:16]),
           reads=(RP(PR),), writes=(r_const,))
    for (src, dst, n) in ((row_st[:, 288:304], rwsum_b, 16),
                          (row_st[:, 0:128], ebias_b, 128),
                          (row_st[:, 128:256], eps16_b, 128),
                          (row_st[:, 256:288], epsg_b, 32)):
        mm_group(PR, [(ones_r, src)], reads=(r_const,), out_ap=PR[:, 0:n])
        p.emit("vector",
               lambda e, dst=dst, n=n: e.tensor_copy(dst, PR[:, 0:n]),
               reads=(RP(PR),), writes=(r_const,))

    xv = x.rearrange("(g p) d -> g p d", p=128)
    outv = out.rearrange("(q t p) d -> q p t d", p=128, t=NT)

    gu_banks = [PG, PU, PD0, PD1]

    big4f = big4.bitcast(F32)
    zview = zbuf.rearrange("p (e t) -> p e t", t=NT)

    def emit_stats_router(qq, src, sink):
        par = qq % 2
        lnb = 48 * par
        lnr = r_ln2[par]

        for t in range(NT):
            def f_st(t=t):
                ap, rds = src(t)
                p.emit("vector", lambda e, ap=ap, t=t: e.bn_stats(
                    stbuf[:, 12 * t:12 * t + 6], ap[:, 0:512]),
                    reads=rds, writes=(r_st,))
                p.emit("vector", lambda e, ap=ap, t=t: e.bn_stats(
                    stbuf[:, 12 * t + 6:12 * t + 12], ap[:, 512:1024]),
                    reads=rds, writes=(r_st,))
            sink.append(f_st)

        def f_smalls():
            for t in range(NT):
                p.emit("vector", lambda e, t=t: e.bn_aggr(
                    lnscr[:, lnb + 2 * t:lnb + 2 * t + 2],
                    stbuf[:, 12 * t:12 * t + 12]),
                    reads=(r_st,), writes=(lnr,))
            mv = lnscr[:, lnb:lnb + 16].rearrange(
                "p (t two) -> p two t", two=2)
            mean8, var8 = mv[:, 0, :], mv[:, 1, :]
            p.emit("vector", lambda e: e.tensor_scalar(
                lnscr[:, lnb + 40:lnb + 48], var8, 1e-5, None, op0=ALU.add),
                reads=(lnr,), writes=(lnr,))
            p.emit("vector", lambda e: e.reciprocal(
                lnscr[:, lnb + 16:lnb + 24], lnscr[:, lnb + 40:lnb + 48]),
                reads=(lnr,), writes=(lnr,))
            p.emit("scalar", lambda e: e.activation(
                lnscr[:, lnb + 16:lnb + 24], lnscr[:, lnb + 16:lnb + 24],
                AF.Sqrt),
                reads=(lnr,), writes=(lnr,))
            p.emit("vector", lambda e: e.tensor_tensor(
                lnscr[:, lnb + 24:lnb + 32], mean8,
                lnscr[:, lnb + 16:lnb + 24], op=ALU.mult),
                reads=(lnr,), writes=(lnr,))
            p.emit("vector", lambda e: e.tensor_scalar(
                lnscr[:, lnb + 24:lnb + 32], lnscr[:, lnb + 24:lnb + 32],
                -1.0, None, op0=ALU.mult),
                reads=(lnr,), writes=(lnr,))
            p.emit("vector", lambda e: e.tensor_scalar(
                lnscr[:, lnb + 32:lnb + 40], mean8, -1.0, None, op0=ALU.mult),
                reads=(lnr,), writes=(lnr,))
        sink.append(f_smalls)

        for t in range(NT):
            def f_rt(t=t):
                g = qq * NT + t
                xpar = t % 2
                xts = xt_sb[:, xpar * D:(xpar + 1) * D]
                p.emit("sync", lambda e, xts=xts, g=g: e.dma_start(
                    out=xts.rearrange("p (k j) -> p k j", k=8),
                    in_=xTt[g].rearrange("(k p) j -> p k j", p=128)),
                    writes=(r_xts[xpar],))
                mm_group(PR, [(
                    xt_sb[:, xpar * D + 128 * k:xpar * D + 128 * k + 128],
                    rwT_sb[:, 16 * k:16 * k + 16]) for k in range(8)],
                    reads=(r_xts[xpar], r_const), out_ap=PR[:, 0:16])
                p.emit("vector", lambda e, t=t: e.scalar_tensor_tensor(
                    rt16, rwsum_b, lnscr[:, lnb + 32 + t:lnb + 33 + t],
                    PR[:, 0:16], op0=ALU.mult, op1=ALU.add),
                    reads=(RP(PR), r_const, lnr), writes=(r_rt16,))
                p.emit("vector", lambda e, t=t: e.tensor_scalar(
                    zview[:, :, t], rt16, lnscr[:, lnb + 16 + t:lnb + 17 + t],
                    None, op0=ALU.mult),
                    reads=(r_rt16, lnr), writes=(r_z,))
            sink.append(f_rt)

        def f_routing():
            p.emit("scalar", lambda e: e.activation(scores, zbuf, AF.Sigmoid),
                   reads=(r_z,), writes=(r_scores,))
            scv = r1.rearrange("p (g j t) -> p g j t", g=4, j=4)
            p.emit("vector", lambda e: e.tensor_tensor(
                r1, scores, ebias_b, op=ALU.add),
                reads=(r_scores, r_const), writes=(r_route,))
            pairs = [(0, 1), (0, 2), (0, 3), (1, 2), (1, 3), (2, 3)]
            for i, (ja, jb) in enumerate(pairs):
                dst = gsb if i == 0 else gscr[:, 0:32]
                p.emit("vector",
                       lambda e, ja=ja, jb=jb, dst=dst: e.tensor_tensor(
                           dst.rearrange("p (g t) -> p g t", g=4),
                           scv[:, :, ja, :], scv[:, :, jb, :], op=ALU.add),
                       reads=(r_route,), writes=(r_route,))
                if i > 0:
                    p.emit("vector", lambda e: e.tensor_tensor(
                        gsb, gsb, gscr[:, 0:32], op=ALU.max),
                        reads=(r_route,), writes=(r_route,))
            p.emit("vector", lambda e: e.tensor_tensor(
                gsb, gsb, epsg_b, op=ALU.add),
                reads=(r_route, r_const), writes=(r_route,))
            gv2 = gsb.rearrange("p (h two t) -> p h two t", h=2, two=2)
            p.emit("vector", lambda e: e.tensor_tensor(
                gscr[:, 0:16].rearrange("p (h t) -> p h t", h=2),
                gv2[:, :, 0, :], gv2[:, :, 1, :], op=ALU.max),
                reads=(r_route,), writes=(r_route,))
            p.emit("vector", lambda e: e.tensor_tensor(
                gscr[:, 16:32].rearrange("p (h t) -> p h t", h=2),
                gv2[:, :, 0, :], gv2[:, :, 1, :], op=ALU.min),
                reads=(r_route,), writes=(r_route,))
            p.emit("vector", lambda e: e.tensor_tensor(
                gscr[:, 32:40], gscr[:, 0:8], gscr[:, 8:16], op=ALU.min),
                reads=(r_route,), writes=(r_route,))
            p.emit("vector", lambda e: e.tensor_tensor(
                gscr[:, 40:48], gscr[:, 16:24], gscr[:, 24:32], op=ALU.max),
                reads=(r_route,), writes=(r_route,))
            p.emit("vector", lambda e: e.tensor_tensor(
                m2b, gscr[:, 32:40], gscr[:, 40:48], op=ALU.max),
                reads=(r_route,), writes=(r_route,))
            for gidx in range(4):
                p.emit("vector", lambda e, gidx=gidx: e.tensor_tensor(
                    gscr[:, 8 * gidx:8 * gidx + 8],
                    gsb[:, 8 * gidx:8 * gidx + 8], m2b, op=ALU.is_ge),
                    reads=(r_route,), writes=(r_route,))
            for ei in range(E):
                p.emit("vector", lambda e, ei=ei: e.tensor_tensor(
                    r2[:, 8 * ei:8 * ei + 8], r1[:, 8 * ei:8 * ei + 8],
                    gscr[:, 8 * (ei // 4):8 * (ei // 4) + 8], op=ALU.mult),
                    reads=(r_route,), writes=(r_route,))
            p.emit("vector", lambda e: e.tensor_tensor(
                r3, r2, eps16_b, op=ALU.add),
                reads=(r_route, r_const), writes=(r_route,))
            kview = r3.rearrange("p (e t) -> p e t", t=NT)
            mview = r1.rearrange("p (e t) -> p e t", t=NT)
            for t in range(NT):
                p.emit("vector", lambda e, t=t: e.max(out8, kview[:, :, t]),
                       reads=(r_route,), writes=(r_route,))
                p.emit("vector", lambda e, t=t: e.tensor_scalar(
                    mview[:, :, t], kview[:, :, t], out8[:, 3:4], None,
                    op0=ALU.is_ge),
                    reads=(r_route,), writes=(r_route,))
            p.emit("vector", lambda e: e.tensor_tensor(
                r2, r1, scores, op=ALU.mult),
                reads=(r_route, r_scores), writes=(r_route,))
            msview = r2.rearrange("p (e t) -> p e t", t=NT)
            for t in range(NT):
                p.emit("vector", lambda e, t=t: e.reduce_sum(
                    rsb[:, t:t + 1], msview[:, :, t], axis=AX.X),
                    reads=(r_route,), writes=(r_route,))
            p.emit("vector", lambda e: e.tensor_scalar(
                rsb, rsb, 1e-20, None, op0=ALU.add),
                reads=(r_route,), writes=(r_route,))
            p.emit("vector", lambda e: e.reciprocal(rrec, rsb),
                   reads=(r_route,), writes=(r_route,))
            cview = comb[:, 128 * par:128 * par + 128].rearrange(
                "p (e t) -> p e t", t=NT)
            for t in range(NT):
                p.emit("vector", lambda e, t=t: e.tensor_scalar(
                    cview[:, :, t], msview[:, :, t], rrec[:, t:t + 1], 2.5,
                    op0=ALU.mult, op1=ALU.mult),
                    reads=(r_route,), writes=(r_comb2[par],))
        sink.append(f_routing)

    def emit_xn_tiles(qq, src, sink=None):
        par_q = qq % 2
        lnb = 48 * par_q
        lnr = r_ln2[par_q]
        for t in range(NT):
            def f_xn(t=t):
                par = t % 2
                sap, rds = src(t)
                xnn = xn_nat[:, par * D:(par + 1) * D]
                p.emit("scalar",
                       lambda e, sap=sap, xnn=xnn, t=t: e.activation(
                           xnn, sap, AF.Identity,
                           bias=lnscr[:, lnb + 24 + t:lnb + 25 + t],
                           scale=lnscr[:, lnb + 16 + t:lnb + 17 + t]),
                       reads=rds + (lnr,), writes=(r_xnn[par],))
                for k in range(8):
                    p.emit("tensor",
                           lambda e, xnn=xnn, k=k: e.transpose(
                               PB[:, 128 * k:128 * k + 128],
                               xnn[:, 128 * k:128 * k + 128], identb),
                           reads=(r_xnn[par], r_const), writes=(RP(PB),))
                dst = xnT.rearrange("p (k t2) -> p k t2", k=8)[
                    :, :, 128 * t:128 * t + 128]
                p.emit("scalar", lambda e, dst=dst: e.activation(
                    dst, PB.rearrange("p (k t2) -> p k t2", k=8),
                    AF.Identity),
                    reads=(RP(PB),), writes=(r_xnT,))
            if sink is None:
                f_xn()
            else:
                sink.append(f_xn)

    def src_acc(t):
        return acc[:, t * D:(t + 1) * D], (r_acc[t],)

    def src_stage(t):
        return (big4f[:, 1024 * t:1024 * t + 1024],
                (r_big[2 * t], r_big[2 * t + 1]))

    for q in range(NQ):
        for t in range(NT):
            g = q * NT + t
            a_t = acc[:, t * D:(t + 1) * D]
            p.emit("sync",
                   lambda e, a_t=a_t, g=g: e.dma_start(out=a_t, in_=xv[g]),
                   writes=(r_acc[t],))
        if DO_ATT:
            for hw in range(2):
                p.emit("gpsimd", lambda e, hw=hw: e.dma_start(
                    out=warea[:, 4096 * hw:4096 * hw + 4096].rearrange(
                        "p (k j) -> p k j", k=8),
                    in_=wqkv[:, 512 * hw:512 * hw + 512].rearrange(
                        "(k p) j -> p k j", p=128)),
                    writes=(r_wblk[hw],))
                p.emit("gpsimd", lambda e, hw=hw: e.dma_start(
                    out=warea[:, 8192 + 4096 * hw:8192 + 4096 * hw + 4096]
                    .rearrange("p (k j) -> p k j", k=8),
                    in_=wqkv[:, 1024 + 512 * hw:1024 + 512 * hw + 512]
                    .rearrange("(k p) j -> p k j", p=128)),
                    writes=(r_wblk[2 + hw],))
            p.emit("gpsimd", lambda e: e.dma_start(
                out=warea[:, 16384:24576].rearrange("p (k j) -> p k j", k=8),
                in_=wqkv[:, 2048:3072].rearrange("(k p) j -> p k j", p=128)),
                writes=(r_wblk[4], r_wblk[5]))

        if q == 0:
            inline = []
            emit_stats_router(0, src_acc, inline)
            for f in inline:
                f()
            emit_xn_tiles(0, src_acc)

        if DO_ATT:
            bank_i = 0
            for jt in range(16):
                wr = ((r_wblk[(jt % 8) // 4],) if jt < 8
                      else (r_wblk[2 + (jt % 8) // 4],))
                for c in range(2):
                    bank = gu_banks[bank_i % 4]
                    bank_i += 1
                    base_w = ((0 if jt < 8 else 8192)
                              + 4096 * ((jt % 8) // 4) + 128 * (jt % 4))
                    mm_group(bank, [(
                        warea[:, base_w + 512 * k:base_w + 512 * k + 128],
                        xnT[:, TQ * k + 512 * c:TQ * k + 512 * c + 512])
                        for k in range(8)],
                        reads=wr + (r_xnT,))
                    dst = big4[:, TQ * jt + 512 * c:TQ * jt + 512 * c + 512]
                    p.emit("scalar", lambda e, dst=dst, bank=bank:
                           e.activation(dst, bank, AF.Identity),
                           reads=(RP(bank),), writes=(r_big[jt],))
            for t in range(NT):
                pr_blk = t // 2
                p0 = 64 * (t % 2)
                for dv in range(2):
                    bank = gu_banks[bank_i % 4]
                    bank_i += 1
                    base_t = 128 * t
                    mm_group(bank, [(
                        xnT[:, TQ * k + base_t:TQ * k + base_t + 64:2],
                        warea[:, 16384 + 1024 * k + 512 * dv:
                              16384 + 1024 * k + 512 * dv + 512])
                        for k in range(8)],
                        reads=(r_wblk[4], r_wblk[5], r_xnT),
                        out_ap=bank[0:32, :])
                    mm_group(bank, [(
                        xnT[:, TQ * k + base_t + 65:TQ * k + base_t + 128:2],
                        warea[:, 16384 + 1024 * k + 512 * dv:
                              16384 + 1024 * k + 512 * dv + 512])
                        for k in range(8)],
                        reads=(r_wblk[4], r_wblk[5], r_xnT),
                        out_ap=bank[32:64, :], tile_position=(0, 32))
                    dstv = v_sb[p0:p0 + 64,
                                TQ * pr_blk + 512 * dv:
                                TQ * pr_blk + 512 * dv + 512]
                    p.emit("scalar",
                           lambda e, dstv=dstv, bank=bank: e.activation(
                               dstv, bank[0:64, :], AF.Identity),
                           reads=(RP(bank),), writes=(r_vsb[pr_blk],))

        mainline = []

        def ML(est, fn):
            mainline.append((est, fn))

        ob_rot = [PS, PD1]

        def emit_gu_item(lhs_g_fn, lhs_u_fn, rd_g, rd_u, dsth, hb_res,
                         gbank, ubank, spar):
            def fn():
                mm_group(gbank, lhs_g_fn(), reads=rd_g)
                mm_group(ubank, lhs_u_fn(), reads=rd_u)
                sil = sil_sb[:, 512 * spar:512 * spar + 512]
                ub = ub_sb[:, 512 * spar:512 * spar + 512]
                p.emit("scalar", lambda e, sil=sil: e.activation(
                    sil, gbank, AF.Silu),
                    reads=(RP(gbank),), writes=(r_sil[spar],))
                p.emit("scalar", lambda e, ub=ub: e.activation(
                    ub, ubank, AF.Identity),
                    reads=(RP(ubank),), writes=(r_ub[spar],))
                p.emit("vector", lambda e, sil=sil, ub=ub: e.tensor_tensor(
                    dsth, sil, ub, op=ALU.mult),
                    reads=(r_sil[spar], r_ub[spar]), writes=(hb_res,))
            ML(3600, fn)

        def emit_down_item(bank, lhsrhs, reads, t, evac):
            def fn():
                mm_group(bank, lhsrhs, reads=reads)
                evac(bank)
            ML(len(lhsrhs) * 230 + 200, fn)

        if DO_SH:
            for half in range(2):
                b0 = 3 * half
                sgo, suo, sdo = 4096 * b0, 4096 * (b0 + 1), 4096 * (b0 + 2)

                def fn_dma(half=half, sgo=sgo, suo=suo, sdo=sdo, b0=b0):
                    p.emit("gpsimd", lambda e: e.dma_start(
                        out=warea[:, sgo:sgo + 4096].rearrange(
                            "p (k j) -> p k j", k=8),
                        in_=shgT[:, 512 * half:512 * half + 512].rearrange(
                            "(k p) j -> p k j", p=128)),
                        writes=(r_wblk[b0],))
                    p.emit("gpsimd", lambda e: e.dma_start(
                        out=warea[:, suo:suo + 4096].rearrange(
                            "p (k j) -> p k j", k=8),
                        in_=shuT[:, 512 * half:512 * half + 512].rearrange(
                            "(k p) j -> p k j", p=128)),
                        writes=(r_wblk[b0 + 1],))
                    p.emit("gpsimd", lambda e: e.dma_start(
                        out=warea[:, sdo:sdo + 4096].rearrange(
                            "p (k j) -> p k j", k=4),
                        in_=shdT[512 * half:512 * half + 512, :].rearrange(
                            "(k p) j -> p k j", p=128)),
                        writes=(r_wblk[b0 + 2],))
                ML(0, fn_dma)
                gu_i = 0
                for icc in range(4):
                    for c in range(2):
                        gbank = gu_banks[gu_i % 4]
                        ubank = gu_banks[(gu_i + 1) % 4]
                        gu_i += 2
                        spar = (2 * icc + c) % 2
                        dsth = hbuf[:, 1024 * icc + 512 * c:
                                    1024 * icc + 512 * c + 512]
                        emit_gu_item(
                            lambda icc=icc, c=c, sgo=sgo: [(
                                warea[:, sgo + 512 * k + 128 * icc:
                                      sgo + 512 * k + 128 * icc + 128],
                                xnT[:, TQ * k + 512 * c:TQ * k + 512 * c + 512])
                                for k in range(8)],
                            lambda icc=icc, c=c, suo=suo: [(
                                warea[:, suo + 512 * k + 128 * icc:
                                      suo + 512 * k + 128 * icc + 128],
                                xnT[:, TQ * k + 512 * c:TQ * k + 512 * c + 512])
                                for k in range(8)],
                            (r_wblk[b0], r_xnT), (r_wblk[b0 + 1], r_xnT),
                            dsth, r_hb[icc], gbank, ubank, spar)
                for t in range(NT):
                    for dd in range(2):
                        bank = ob_rot[dd % 2]

                        def evac(bank, t=t, dd=dd):
                            a_sl = acc[:, D * t + 512 * dd:
                                       D * t + 512 * dd + 512]
                            p.emit("vector", lambda e, a_sl=a_sl, bank=bank:
                                   e.tensor_tensor(a_sl, a_sl, bank,
                                                   op=ALU.add),
                                   reads=(RP(bank),), writes=(r_acc[t],))
                        emit_down_item(
                            bank,
                            [(hbuf[:, 1024 * icc2 + 128 * t:
                                   1024 * icc2 + 128 * t + 128],
                              warea[:, sdo + 1024 * icc2 + 512 * dd:
                                    sdo + 1024 * icc2 + 512 * dd + 512])
                             for icc2 in range(4)],
                            (r_hb[0], r_hb[1], r_hb[2], r_hb[3],
                             r_wblk[b0 + 2]),
                            t, evac)

        if DO_MOE:
            def emit_exp_dma_ml(ei):
                slot = ei % 2
                sofs = 12288 * slot
                sblks = (r_wblk[3 * slot], r_wblk[3 * slot + 1],
                         r_wblk[3 * slot + 2])

                def fn(sofs=sofs, ei=ei, sblks=sblks):
                    p.emit("gpsimd", lambda e: e.dma_start(
                        out=warea[:, sofs:sofs + 4096].rearrange(
                            "p (k i) -> p k i", k=8),
                        in_=gwT[ei].rearrange("(k p) i -> p k i", p=128)),
                        writes=(sblks[0],))
                    p.emit("gpsimd", lambda e: e.dma_start(
                        out=warea[:, sofs + 4096:sofs + 8192].rearrange(
                            "p (k i) -> p k i", k=8),
                        in_=uwT[ei].rearrange("(k p) i -> p k i", p=128)),
                        writes=(sblks[1],))
                    p.emit("gpsimd", lambda e: e.dma_start(
                        out=warea[:, sofs + 8192:sofs + 12288].rearrange(
                            "p (k h) -> p k h", k=4),
                        in_=dwT[ei].rearrange("(k p) h -> p k h", p=128)),
                        writes=(sblks[2],))
                ML(0, fn)

            emit_exp_dma_ml(0)
            emit_exp_dma_ml(1)
            cbv = comb[:, 128 * (q % 2):128 * (q % 2) + 128].rearrange(
                "p (e2 t) -> p e2 t", t=NT)
            for ei in range(E):
                slot = ei % 2
                sofs = 12288 * slot
                sblks = (r_wblk[3 * slot], r_wblk[3 * slot + 1],
                         r_wblk[3 * slot + 2])
                gu_i = 0
                for ic in range(4):
                    for c in range(2):
                        gbank = gu_banks[gu_i % 4]
                        ubank = gu_banks[(gu_i + 1) % 4]
                        gu_i += 2
                        spar = (2 * ic + c) % 2
                        dsth = hbuf[:, 1024 * ic + 512 * c:
                                    1024 * ic + 512 * c + 512]
                        emit_gu_item(
                            lambda ic=ic, c=c, sofs=sofs: [(
                                warea[:, sofs + 512 * k + 128 * ic:
                                      sofs + 512 * k + 128 * ic + 128],
                                xnT[:, TQ * k + 512 * c:TQ * k + 512 * c + 512])
                                for k in range(8)],
                            lambda ic=ic, c=c, sofs=sofs: [(
                                warea[:, sofs + 4096 + 512 * k + 128 * ic:
                                      sofs + 4096 + 512 * k + 128 * ic + 128],
                                xnT[:, TQ * k + 512 * c:TQ * k + 512 * c + 512])
                                for k in range(8)],
                            (sblks[0], r_xnT), (sblks[1], r_xnT),
                            dsth, r_hb[ic], gbank, ubank, spar)
                rot = ob_rot if ei < 4 else [PS, PD1, PAV]
                for t in range(NT):
                    for dd in range(2):
                        bank = rot[(2 * t + dd) % len(rot)]

                        def evac(bank, t=t, dd=dd, ei=ei, cbv=cbv, q=q):
                            a_sl = acc[:, D * t + 512 * dd:
                                       D * t + 512 * dd + 512]
                            p.emit("vector",
                                   lambda e, a_sl=a_sl, bank=bank, ei=ei,
                                   t=t, cbv=cbv:
                                   e.scalar_tensor_tensor(
                                       a_sl, bank, cbv[:, ei, t:t + 1], a_sl,
                                       op0=ALU.mult, op1=ALU.add),
                                   reads=(RP(bank), r_comb2[q % 2]),
                                   writes=(r_acc[t],))
                        emit_down_item(
                            bank,
                            [(hbuf[:, 1024 * ic2 + 128 * t:
                                   1024 * ic2 + 128 * t + 128],
                              warea[:, sofs + 8192 + 1024 * ic2 + 512 * dd:
                                    sofs + 8192 + 1024 * ic2 + 512 * dd + 512])
                             for ic2 in range(4)],
                            (r_hb[0], r_hb[1], r_hb[2], r_hb[3], sblks[2]),
                            t, evac)
                if ei + 2 < E:
                    emit_exp_dma_ml(ei + 2)

        att_steps = []

        if DO_ATT and ATT_LVL >= 3:
            for h in range(NH):
                hp = h // 2
                hpar = h % 2
                spar = h % 2
                qjt = hp
                kjt = 8 + hp
                q_r0 = 64 * hpar

                def fn_scores(h=h, hp=hp, hpar=hpar, spar=spar, qjt=qjt,
                              kjt=kjt, q_r0=q_r0):
                    for n in range(NT):
                        lhs = big4[q_r0:q_r0 + 64,
                                   TQ * qjt + 128 * n:TQ * qjt + 128 * n + 128]
                        kbase = TQ * kjt + 128 * n
                        rhs_a = big4[q_r0:q_r0 + 64, kbase:kbase + 64:2]
                        rhs_b = big4[q_r0:q_r0 + 64,
                                     kbase + 65:kbase + 128:2]
                        p.emit("tensor",
                               lambda e, lhs=lhs, rhs_a=rhs_a, n=n: e.matmul(
                                   PR[:, 64 * n:64 * n + 32], lhs, rhs_a,
                                   start=True, stop=True),
                               reads=(r_big[qjt], r_big[kjt]),
                               writes=(RP(PR),))
                        p.emit("tensor",
                               lambda e, lhs=lhs, rhs_b=rhs_b, n=n: e.matmul(
                                   PR[:, 64 * n + 32:64 * n + 64], lhs,
                                   rhs_b, start=True, stop=True),
                               reads=(r_big[qjt], r_big[kjt]),
                               writes=(RP(PR),))
                    smb = sm8[:, 32 * spar:32 * spar + 32]
                    m1, nb1 = smb[:, 0:1], smb[:, 1:2]
                    s8, r8 = smb[:, 16:24], smb[:, 24:32]
                    p.emit("vector", lambda e, m1=m1: e.reduce_max(
                        m1, PR, axis=AX.X),
                        reads=(RP(PR),), writes=(r_sm8[spar],))
                    p.emit("vector", lambda e, m1=m1, nb1=nb1: e.tensor_scalar(
                        nb1, m1, -0.125, None, op0=ALU.mult),
                        reads=(r_sm8[spar],), writes=(r_sm8[spar],))
                    expd = exp_sb[:, 512 * spar:512 * spar + 512]
                    p.emit("scalar", lambda e, expd=expd, nb1=nb1:
                           e.activation(expd, PR, AF.Exp,
                                        bias=nb1, scale=0.125),
                           reads=(RP(PR), r_sm8[spar]),
                           writes=(r_exps[spar],))
                    e3 = expd.rearrange("p (n k) -> p n k", n=8)
                    p.emit("vector", lambda e, e3=e3, s8=s8: e.reduce_sum(
                        s8, e3, axis=AX.X),
                        reads=(r_exps[spar],), writes=(r_sm8[spar],))
                    p.emit("vector", lambda e, s8=s8, r8=r8: e.reciprocal(
                        r8, s8),
                        reads=(r_sm8[spar],), writes=(r_sm8[spar],))
                    attnd = attn_sb[:, 512 * spar:512 * spar + 512]
                    for n in range(NT):
                        p.emit("vector",
                               lambda e, n=n, attnd=attnd, expd=expd, r8=r8:
                               e.tensor_scalar(
                                   attnd[:, 64 * n:64 * n + 64],
                                   expd[:, 64 * n:64 * n + 64],
                                   r8[:, n:n + 1], None, op0=ALU.mult),
                               reads=(r_exps[spar], r_sm8[spar]),
                               writes=(r_attns[spar],))
                att_steps.append((12000, fn_scores))

                if ATT_LVL >= 4:
                    def fn_tr(h=h, spar=spar):
                        attnd = attn_sb[:, 512 * spar:512 * spar + 512]
                        for pr2 in range(4):
                            p.emit("tensor",
                                   lambda e, pr2=pr2, attnd=attnd:
                                   e.transpose(
                                       PB[:, 128 * pr2:128 * pr2 + 128],
                                       attnd[:, 128 * pr2:128 * pr2 + 128],
                                       identb),
                                   reads=(r_attns[spar], r_const),
                                   writes=(RP(PB),))
                        attnTd = attnT_sb[:, 512 * spar:512 * spar + 512]
                        p.emit("scalar", lambda e, attnTd=attnTd:
                               e.activation(attnTd, PB[:, 0:512], AF.Identity),
                               reads=(RP(PB),), writes=(r_attnT[spar],))
                    att_steps.append((2500, fn_tr))

                if ATT_LVL >= 5:
                    def fn_av(h=h, hp=hp, hpar=hpar, spar=spar):
                        attnTd = attnT_sb[:, 512 * spar:512 * spar + 512]
                        for n in range(NT):
                            prb = n // 2
                            vr0 = 64 * (n % 2)
                            av_bank = PR if n % 2 == 0 else PAV
                            lhsv = v_sb[vr0:vr0 + 64,
                                        TQ * prb + 64 * h:
                                        TQ * prb + 64 * h + 64]
                            rhsa = attnTd[vr0:vr0 + 64,
                                          128 * prb:128 * prb + 128]
                            p.emit("tensor",
                                   lambda e, lhsv=lhsv, rhsa=rhsa, n=n,
                                   hpar=hpar, av_bank=av_bank: e.matmul(
                                       av_bank[64 * hpar:64 * hpar + 64,
                                               128 * (n // 2):
                                               128 * (n // 2) + 128],
                                       lhsv, rhsa, start=True, stop=True,
                                       tile_position=(64 * (n % 2),
                                                      64 * hpar)),
                                   reads=(r_vsb[prb], r_attnT[spar]),
                                   writes=(RP(av_bank),))
                        r0 = 64 * hpar
                        for par2 in range(2):
                            av_bank = PR if par2 == 0 else PAV
                            base = TQ * hp + 128 * par2
                            dsta = _four_block_cols(aoT, base, 256, 128)
                            p.emit("scalar",
                                   lambda e, dsta=dsta, av_bank=av_bank,
                                   r0=r0: e.activation(
                                       dsta[r0:r0 + 64],
                                       av_bank[r0:r0 + 64, 0:512],
                                       AF.Identity),
                                   reads=(RP(av_bank),),
                                   writes=(r_aoT[hp],))
                    att_steps.append((1500, fn_av))

        pa_items = []
        if q + 1 < NQ:
            for t in range(NT):
                g2 = (q + 1) * NT + t

                def f_stage(t=t, g2=g2):
                    p.emit("sync", lambda e, t=t, g2=g2: e.dma_start(
                        out=big4f[:, 1024 * t:1024 * t + 1024], in_=xv[g2]),
                        writes=(r_big[2 * t], r_big[2 * t + 1]))
                pa_items.append(f_stage)
            emit_stats_router(q + 1, src_stage, pa_items)

        ml_pos = [0]

        def pump(ns):
            acc_ns = 0
            while ml_pos[0] < len(mainline) and acc_ns < ns:
                est, fn = mainline[ml_pos[0]]
                ml_pos[0] += 1
                fn()
                acc_ns += est

        for i, (fill, fn) in enumerate(att_steps):
            fn()
            pump(fill if i >= 4 else 2000)
        for f in pa_items:
            f()
            pump(6000)
        while ml_pos[0] < len(mainline):
            mainline[ml_pos[0]][1]()
            ml_pos[0] += 1

        ay_items = []
        if DO_ATT and ATT_LVL >= 6:
            p.emit("gpsimd", lambda e: e.dma_start(
                out=warea[:, 8192:16384].rearrange("p (k j) -> p k j", k=8),
                in_=wout.rearrange("(k p) j -> p k j", p=128)),
                writes=(r_wblk[2], r_wblk[3]))
            for t in range(NT):
                for dd in range(2):
                    bank = ob_rot[dd % 2]

                    def f_ay(t=t, dd=dd, bank=bank):
                        mm_group(bank, [(
                            aoT[:, TQ * jt2 + 128 * t:
                                TQ * jt2 + 128 * t + 128],
                            warea[:, 8192 + 1024 * jt2 + 512 * dd:
                                  8192 + 1024 * jt2 + 512 * dd + 512])
                            for jt2 in range(8)],
                            reads=(r_aoT[0], r_aoT[1], r_aoT[2], r_aoT[3],
                                   r_aoT[4], r_aoT[5], r_aoT[6], r_aoT[7],
                                   r_wblk[2], r_wblk[3]))
                        a_sl = acc[:, D * t + 512 * dd:
                                   D * t + 512 * dd + 512]
                        p.emit("vector", lambda e, a_sl=a_sl, bank=bank:
                               e.tensor_tensor(a_sl, a_sl, bank, op=ALU.add),
                               reads=(RP(bank),), writes=(r_acc[t],))
                    ay_items.append(f_ay)
        xn_items = []
        if q + 1 < NQ:
            emit_xn_tiles(q + 1, src_stage, sink=xn_items)
        tails = [ay_items, xn_items]
        while any(tails):
            for tl in tails:
                if tl:
                    tl.pop(0)()

        p.emit("sync", lambda e, q=q: e.dma_start(
            out=outv[q], in_=acc.rearrange("p (t d) -> p t d", t=NT)),
            reads=tuple(r_acc), writes=(r_out,))

    p.emit("vector", lambda e: e.nop(), reads=(r_out,))

    import contextlib
    with contextlib.ExitStack() as stack:
        sems = {}
        for e in ENGINES:
            sems[e] = stack.enter_context(nc.semaphore(name=f"s_{e}"))
        block = stack.enter_context(nc.Block())
        p.finalize(block, sems)
    return nc


def _two_block_cols(buf, base, stride, width):
    v = buf[:, base:base + stride + 1:stride]
    w = buf[:, base:base + width]
    u = v.copy()
    u.ap = v.ap + w.ap[1:]
    return u


def _four_block_cols(buf, base, stride, width):
    v = buf[:, base:base + 3 * stride + 1:stride]
    w = buf[:, base:base + width]
    u = v.copy()
    u.ap = v.ap + w.ap[1:]
    return u


def _ksub_cols(buf, base):
    v = buf[:, base:base + 66:65]
    w = buf[:, base:base + 64:2]
    u = v.copy()
    u.ap = v.ap + w.ap[1:]
    return u


def _ksub_ap(big4, r0, kbase):
    v = big4[r0:r0 + 64, kbase:kbase + 66:65]
    w = big4[r0:r0 + 64, kbase:kbase + 64:2]
    u = v.copy()
    u.ap = v.ap + w.ap[1:]
    return u


_NC_CACHE = None


def kernel(L0_attn_x, L0_attn_W_qkv, L0_attn_W_out, L0_ffn_router_weight,
           L0_ffn_e_bias, L0_ffn_gate_w, L0_ffn_up_w, L0_ffn_down_w,
           L0_ffn_sh_gate_w, L0_ffn_sh_up_w, L0_ffn_sh_down_w):
    global _NC_CACHE
    f32 = np.float32

    def A(v):
        return np.ascontiguousarray(np.asarray(v, dtype=f32))

    x = A(L0_attn_x)
    wqkv = A(L0_attn_W_qkv)
    wout = A(L0_attn_W_out)
    rwT = A(np.asarray(L0_ffn_router_weight, f32).T)
    eb = np.asarray(L0_ffn_e_bias, f32)
    ebias_rep = A(np.repeat(eb, NT)[None, :])
    eps16_rep = A(np.repeat((15 - np.arange(E)) * 1e-30, NT)[None, :])
    epsg_rep = A(np.repeat((3 - np.arange(4)) * 1e-30, NT)[None, :])
    gwT = A(np.asarray(L0_ffn_gate_w, f32).transpose(0, 2, 1))
    uwT = A(np.asarray(L0_ffn_up_w, f32).transpose(0, 2, 1))
    dwT = A(np.asarray(L0_ffn_down_w, f32).transpose(0, 2, 1))
    shgT = A(np.asarray(L0_ffn_sh_gate_w, f32).T)
    shuT = A(np.asarray(L0_ffn_sh_up_w, f32).T)
    shdT = A(np.asarray(L0_ffn_sh_down_w, f32).T)
    identf = np.eye(128, dtype=f32)

    if _NC_CACHE is None:
        _NC_CACHE = build_program()
    nc = _NC_CACHE

    in_maps = []
    for c in range(N_CORES):
        xc = np.ascontiguousarray(x[c])
        xTt = np.ascontiguousarray(
            xc.reshape(T // 128, 128, D).transpose(0, 2, 1))
        in_maps.append({
            "x": xc, "xTt": xTt, "wqkv": wqkv, "wout": wout, "rwT": rwT,
            "ebias_rep": ebias_rep, "eps16_rep": eps16_rep,
            "epsg_rep": epsg_rep, "gwT": gwT, "uwT": uwT, "dwT": dwT,
            "shgT": shgT, "shuT": shuT, "shdT": shdT, "identf": identf,
        })
    res = run_bass_kernel_spmd(nc, in_maps, core_ids=list(range(N_CORES)))
    return np.stack([r["out"] for r in res.results], axis=0)


# version 12
# speedup vs baseline: 1.0010x; 1.0010x over previous
import numpy as np
import concourse.bass as bass
import concourse.mybir as mybir
from concourse.bass_utils import run_bass_kernel_spmd

F32 = mybir.dt.float32
BF16 = mybir.dt.bfloat16
AF = mybir.ActivationFunctionType
ALU = mybir.AluOpType
AX = mybir.AxisListType

N_CORES = 8
T = 4096
D = 1024
NQ = 4
TQ = T // NQ
NT = TQ // 128
E = 16
IM = 512
NH = 16
DH = 64

import os as _os
ATT_LVL = int(_os.environ.get("K_ATT", "9"))
DO_ATT = ATT_LVL > 0
DO_SH = _os.environ.get("K_SH", "1") == "1"
DO_MOE = _os.environ.get("K_MOE", "1") == "1"

NDQ = 4
SYNC_QS = [f"sync{i}" for i in range(NDQ)]
GPS_QS = [f"gpsimd{i}" for i in range(NDQ)]
ENGINES = ["tensor", "vector", "scalar"] + SYNC_QS + GPS_QS
DMA_ENGINES = set(SYNC_QS) | set(GPS_QS)
REAL_ENGINE = {e: e for e in ["tensor", "vector", "scalar"]}
REAL_ENGINE.update({q: "sync" for q in SYNC_QS})
REAL_ENGINE.update({q: "gpsimd" for q in GPS_QS})


class Op:
    __slots__ = ("eng", "fn", "deps", "inc_step", "need_inc", "val", "idx")

    def __init__(self, eng, fn, idx):
        self.eng = eng
        self.fn = fn
        self.idx = idx
        self.deps = set()
        self.inc_step = 16 if eng in DMA_ENGINES else 1
        self.need_inc = False
        self.val = None


class Res:
    __slots__ = ("name", "last_write", "readers")

    def __init__(self, name):
        self.name = name
        self.last_write = None
        self.readers = []


class Prog:
    def __init__(self):
        self.ops = []
        self.rr = {"sync": 0, "gpsimd": 0}
        self.qlast = {q: None for q in DMA_ENGINES}

    def emit(self, eng, fn, reads=(), writes=()):
        if eng in ("sync", "gpsimd"):
            qi = self.rr[eng]
            self.rr[eng] = (qi + 1) % NDQ
            eng = f"{eng}{qi}"
        op = Op(eng, fn, len(self.ops))
        for r in reads:
            if r.last_write is not None:
                op.deps.add(r.last_write)
        for w in writes:
            if w.last_write is not None:
                op.deps.add(w.last_write)
            for rd in w.readers:
                op.deps.add(rd)
        if eng in DMA_ENGINES:
            if self.qlast[eng] is not None:
                op.deps.add(self.qlast[eng])
            self.qlast[eng] = op
        op.deps = {d for d in op.deps if d.eng != eng or eng != "tensor"}
        for w in writes:
            w.last_write = op
            w.readers = []
        for r in reads:
            r.readers.append(op)
        self.ops.append(op)
        return op

    def finalize(self, block, sems):
        for op in self.ops:
            if op.eng in DMA_ENGINES:
                op.need_inc = True
            for d in op.deps:
                d.need_inc = True
        counters = {e: 0 for e in ENGINES}
        for op in self.ops:
            if op.need_inc:
                counters[op.eng] += op.inc_step
                op.val = counters[op.eng]
        for real in ["tensor", "vector", "scalar", "sync", "gpsimd"]:
            eng_ops = [o for o in self.ops if REAL_ENGINE[o.eng] == real]

            def body(eng, eng_ops=eng_ops):
                waited = {}
                for op in eng_ops:
                    for d in sorted(op.deps, key=lambda o: o.idx):
                        if d.val is None:
                            continue
                        if waited.get(d.eng, -1) >= d.val:
                            continue
                        eng.wait_ge(sems[d.eng], d.val)
                        waited[d.eng] = d.val
                    inst = op.fn(eng)
                    if op.need_inc:
                        inst.then_inc(sems[op.eng], op.inc_step)

            getattr(block, real)(body)


def build_program():
    nc = bass.Bass("TRN2", target_bir_lowering=False)

    def din(name, shape):
        return nc.dram_tensor(name, shape, F32, kind="ExternalInput").ap()

    x = din("x", [T, D])
    xTt = din("xTt", [T // 128, D, 128])
    wqkv = din("wqkv", [D, 3 * D])
    wout = din("wout", [D, D])
    rwT = din("rwT", [D, E])
    ebias_rep = din("ebias_rep", [1, E * NT])
    eps16_rep = din("eps16_rep", [1, E * NT])
    epsg_rep = din("epsg_rep", [1, 4 * NT])
    gwT = din("gwT", [E, D, IM])
    uwT = din("uwT", [E, D, IM])
    dwT = din("dwT", [E, IM, D])
    shgT = din("shgT", [D, D])
    shuT = din("shuT", [D, D])
    shdT = din("shdT", [D, D])
    identf = din("identf", [128, 128])
    out = nc.dram_tensor("out", [T, D], F32, kind="ExternalOutput").ap()
    DBG = _os.environ.get("K_DBG", "0") == "1"
    if DBG:
        dbg_xnT = nc.dram_tensor("dbg_xnT", [128, 8 * TQ], BF16,
                                 kind="ExternalOutput").ap()
        dbg_r = nc.dram_tensor("dbg_r", [128, 3 * E * NT], F32,
                               kind="ExternalOutput").ap()

    def sb(name, shape, dt=F32):
        return nc.alloc_sbuf_tensor(name, shape, dt).ap()

    acc = sb("acc", [128, NT * D])
    xnT = sb("xnT", [128, 8 * TQ], BF16)
    warea = sb("warea", [128, 6 * 4096], BF16)
    big4 = sb("big4", [128, 16 * TQ], BF16)
    aoT = sb("aoT", [128, 8 * TQ], BF16)
    v_sb = sb("v_sb", [128, 4 * TQ], BF16)
    hbuf = sb("hbuf", [128, 4 * TQ], BF16)
    xn_nat = sb("xn_nat", [128, 2 * D], BF16)
    xt_sb = sb("xt_sb", [128, 2 * D])
    exp_sb = sb("exp_sb", [128, 2 * 512])
    attn_sb = sb("attn_sb", [128, 2 * 512], BF16)
    attnT_sb = sb("attnT_sb", [128, 2 * 512], BF16)
    sil_sb = sb("sil_sb", [128, 2 * 512], BF16)
    stbuf = sb("stbuf", [128, 12 * NT])
    lnscr = sb("lnscr", [128, 96])
    zbuf = sb("zbuf", [128, E * NT])
    scores = sb("scores", [128, E * NT])
    r1 = sb("r1", [128, E * NT])
    r2 = sb("r2", [128, E * NT])
    r3 = sb("r3", [128, E * NT])
    comb = sb("comb", [128, 2 * E * NT])
    gsb = sb("gsb", [128, 4 * NT])
    gscr = sb("gscr", [128, 6 * NT])
    m2b = sb("m2b", [128, NT])
    out8 = sb("out8", [128, 8])
    rsb = sb("rsb", [128, NT])
    rrec = sb("rrec", [128, NT])
    rt16 = sb("rt16", [128, E])
    sm8 = sb("sm8", [128, 8 * NT])
    rwT_sb = sb("rwT_sb", [128, 8 * E])
    rwsum_b = sb("rwsum_b", [128, E])
    ebias_b = sb("ebias_b", [128, E * NT])
    eps16_b = sb("eps16_b", [128, E * NT])
    epsg_b = sb("epsg_b", [128, 4 * NT])
    identb = sb("identb", [128, 128], BF16)
    ones_r = sb("ones_r", [1, 128])
    ones_c = sb("ones_c", [128, 1])
    row_st = sb("row_st", [1, 4 * E * NT])

    PB = nc.alloc_psum_tensor("PB", [128, 1024], BF16).ap()
    PAV = nc.alloc_psum_tensor("PAV", [128, 512], F32).ap()
    PG = nc.alloc_psum_tensor("PG", [128, 512], F32).ap()
    PU = nc.alloc_psum_tensor("PU", [128, 512], F32).ap()
    PD0 = nc.alloc_psum_tensor("PD0", [128, 512], F32).ap()
    PD1 = nc.alloc_psum_tensor("PD1", [128, 512], F32).ap()
    PS = nc.alloc_psum_tensor("PS", [128, 512], F32).ap()
    PR = nc.alloc_psum_tensor("PR", [128, 512], F32).ap()

    p = Prog()

    r_acc = [Res(f"acc{t}") for t in range(NT)]
    r_xnT = Res("xnT")
    r_wblk = [Res(f"wblk{i}") for i in range(6)]
    r_big = [Res(f"big{i}") for i in range(16)]
    r_aoT = [Res(f"aoT{i}") for i in range(8)]
    r_vsb = [Res(f"vsb{i}") for i in range(4)]
    r_hb = [Res(f"hb{i}") for i in range(4)]
    r_xnn = [Res(f"xnn{i}") for i in range(2)]
    r_xts = [Res(f"xts{i}") for i in range(2)]
    r_exps = [Res(f"exps{i}") for i in range(4)]
    r_attns = [Res(f"attns{i}") for i in range(4)]
    r_attnT = [Res(f"attnT{i}") for i in range(4)]
    r_sil = [Res(f"sil{i}") for i in range(2)]
    r_st = Res("stbuf")
    r_ln2 = [Res("lnscr0"), Res("lnscr1")]
    r_z = Res("zbuf")
    r_scores = Res("scores")
    r_route = Res("route_scratch")
    r_comb2 = [Res("comb0"), Res("comb1")]
    r_rt16 = Res("rt16")
    r_sm8 = [Res(f"sm8{i}") for i in range(4)]
    r_out = Res("out_hbm")
    r_pb = {id(PB): Res("PB"), id(PAV): Res("PAV"), id(PG): Res("PG"),
            id(PU): Res("PU"), id(PD0): Res("PD0"), id(PD1): Res("PD1"),
            id(PS): Res("PS"), id(PR): Res("PR")}
    r_const = Res("consts")

    def RP(bank):
        return r_pb[id(bank)]

    def mm_group(bank, lhsrhs, reads, out_ap=None, tile_position=None):
        ob = bank if out_ap is None else out_ap
        n = len(lhsrhs)
        for i, (lh, rh) in enumerate(lhsrhs):
            p.emit("tensor",
                   lambda e, lh=lh, rh=rh, i=i, n=n, ob=ob: e.matmul(
                       ob, lh, rh, start=(i == 0), stop=(i == n - 1),
                       tile_position=tile_position),
                   reads=reads, writes=(RP(bank),))

    p.emit("sync", lambda e: e.dma_start(
        out=rwT_sb.rearrange("p (k e) -> p k e", k=8),
        in_=rwT.rearrange("(k p) e -> p k e", p=128)),
        writes=(r_const,))
    p.emit("gpsimd", lambda e: e.dma_start(out=identb, in_=identf),
           writes=(r_const,))
    p.emit("sync", lambda e: e.dma_start(
        out=row_st[:, 0:128], in_=ebias_rep), writes=(r_const,))
    p.emit("sync", lambda e: e.dma_start(
        out=row_st[:, 128:256], in_=eps16_rep), writes=(r_const,))
    p.emit("sync", lambda e: e.dma_start(
        out=row_st[:, 256:288], in_=epsg_rep), writes=(r_const,))
    p.emit("vector", lambda e: e.memset(ones_r, 1.0), writes=(r_const,))
    p.emit("vector", lambda e: e.memset(ones_c, 1.0), writes=(r_const,))
    mm_group(PR, [(ones_c, rwT_sb[:, 16 * k:16 * k + 16]) for k in range(8)],
             reads=(r_const,), out_ap=PR[0:1, 0:16])
    p.emit("vector", lambda e: e.tensor_copy(row_st[:, 288:304], PR[0:1, 0:16]),
           reads=(RP(PR),), writes=(r_const,))
    for (src, dst, n) in ((row_st[:, 288:304], rwsum_b, 16),
                          (row_st[:, 0:128], ebias_b, 128),
                          (row_st[:, 128:256], eps16_b, 128),
                          (row_st[:, 256:288], epsg_b, 32)):
        mm_group(PR, [(ones_r, src)], reads=(r_const,), out_ap=PR[:, 0:n])
        p.emit("vector",
               lambda e, dst=dst, n=n: e.tensor_copy(dst, PR[:, 0:n]),
               reads=(RP(PR),), writes=(r_const,))

    xv = x.rearrange("(g p) d -> g p d", p=128)
    outv = out.rearrange("(q t p) d -> q p t d", p=128, t=NT)

    gu_banks = [PG, PU, PD0, PD1]

    big4f = big4.bitcast(F32)
    zview = zbuf.rearrange("p (e t) -> p e t", t=NT)

    def emit_stats_router(qq, src, sink):
        par = qq % 2
        lnb = 48 * par
        lnr = r_ln2[par]

        for t in range(NT):
            def f_st(t=t):
                ap, rds = src(t)
                p.emit("vector", lambda e, ap=ap, t=t: e.bn_stats(
                    stbuf[:, 12 * t:12 * t + 6], ap[:, 0:512]),
                    reads=rds, writes=(r_st,))
                p.emit("vector", lambda e, ap=ap, t=t: e.bn_stats(
                    stbuf[:, 12 * t + 6:12 * t + 12], ap[:, 512:1024]),
                    reads=rds, writes=(r_st,))
            sink.append(f_st)

        def f_smalls():
            for t in range(NT):
                p.emit("vector", lambda e, t=t: e.bn_aggr(
                    lnscr[:, lnb + 2 * t:lnb + 2 * t + 2],
                    stbuf[:, 12 * t:12 * t + 12]),
                    reads=(r_st,), writes=(lnr,))
            mv = lnscr[:, lnb:lnb + 16].rearrange(
                "p (t two) -> p two t", two=2)
            mean8, var8 = mv[:, 0, :], mv[:, 1, :]
            p.emit("vector", lambda e: e.tensor_scalar(
                lnscr[:, lnb + 40:lnb + 48], var8, 1e-5, None, op0=ALU.add),
                reads=(lnr,), writes=(lnr,))
            p.emit("vector", lambda e: e.reciprocal(
                lnscr[:, lnb + 16:lnb + 24], lnscr[:, lnb + 40:lnb + 48]),
                reads=(lnr,), writes=(lnr,))
            p.emit("scalar", lambda e: e.activation(
                lnscr[:, lnb + 16:lnb + 24], lnscr[:, lnb + 16:lnb + 24],
                AF.Sqrt),
                reads=(lnr,), writes=(lnr,))
            p.emit("vector", lambda e: e.tensor_tensor(
                lnscr[:, lnb + 24:lnb + 32], mean8,
                lnscr[:, lnb + 16:lnb + 24], op=ALU.mult),
                reads=(lnr,), writes=(lnr,))
            p.emit("vector", lambda e: e.tensor_scalar(
                lnscr[:, lnb + 24:lnb + 32], lnscr[:, lnb + 24:lnb + 32],
                -1.0, None, op0=ALU.mult),
                reads=(lnr,), writes=(lnr,))
            p.emit("vector", lambda e: e.tensor_scalar(
                lnscr[:, lnb + 32:lnb + 40], mean8, -1.0, None, op0=ALU.mult),
                reads=(lnr,), writes=(lnr,))
        sink.append(f_smalls)

        for t in range(NT):
            def f_rt(t=t):
                g = qq * NT + t
                xpar = t % 2
                xts = xt_sb[:, xpar * D:(xpar + 1) * D]
                p.emit("sync", lambda e, xts=xts, g=g: e.dma_start(
                    out=xts.rearrange("p (k j) -> p k j", k=8),
                    in_=xTt[g].rearrange("(k p) j -> p k j", p=128)),
                    writes=(r_xts[xpar],))
                mm_group(PR, [(
                    xt_sb[:, xpar * D + 128 * k:xpar * D + 128 * k + 128],
                    rwT_sb[:, 16 * k:16 * k + 16]) for k in range(8)],
                    reads=(r_xts[xpar], r_const), out_ap=PR[:, 0:16])
                p.emit("vector", lambda e, t=t: e.scalar_tensor_tensor(
                    rt16, rwsum_b, lnscr[:, lnb + 32 + t:lnb + 33 + t],
                    PR[:, 0:16], op0=ALU.mult, op1=ALU.add),
                    reads=(RP(PR), r_const, lnr), writes=(r_rt16,))
                p.emit("vector", lambda e, t=t: e.tensor_scalar(
                    zview[:, :, t], rt16, lnscr[:, lnb + 16 + t:lnb + 17 + t],
                    None, op0=ALU.mult),
                    reads=(r_rt16, lnr), writes=(r_z,))
            sink.append(f_rt)

        def f_routing():
            p.emit("scalar", lambda e: e.activation(scores, zbuf, AF.Sigmoid),
                   reads=(r_z,), writes=(r_scores,))
            scv = r1.rearrange("p (g j t) -> p g j t", g=4, j=4)
            p.emit("vector", lambda e: e.tensor_tensor(
                r1, scores, ebias_b, op=ALU.add),
                reads=(r_scores, r_const), writes=(r_route,))
            pairs = [(0, 1), (0, 2), (0, 3), (1, 2), (1, 3), (2, 3)]
            for i, (ja, jb) in enumerate(pairs):
                dst = gsb if i == 0 else gscr[:, 0:32]
                p.emit("vector",
                       lambda e, ja=ja, jb=jb, dst=dst: e.tensor_tensor(
                           dst.rearrange("p (g t) -> p g t", g=4),
                           scv[:, :, ja, :], scv[:, :, jb, :], op=ALU.add),
                       reads=(r_route,), writes=(r_route,))
                if i > 0:
                    p.emit("vector", lambda e: e.tensor_tensor(
                        gsb, gsb, gscr[:, 0:32], op=ALU.max),
                        reads=(r_route,), writes=(r_route,))
            p.emit("vector", lambda e: e.tensor_tensor(
                gsb, gsb, epsg_b, op=ALU.add),
                reads=(r_route, r_const), writes=(r_route,))
            gv2 = gsb.rearrange("p (h two t) -> p h two t", h=2, two=2)
            p.emit("vector", lambda e: e.tensor_tensor(
                gscr[:, 0:16].rearrange("p (h t) -> p h t", h=2),
                gv2[:, :, 0, :], gv2[:, :, 1, :], op=ALU.max),
                reads=(r_route,), writes=(r_route,))
            p.emit("vector", lambda e: e.tensor_tensor(
                gscr[:, 16:32].rearrange("p (h t) -> p h t", h=2),
                gv2[:, :, 0, :], gv2[:, :, 1, :], op=ALU.min),
                reads=(r_route,), writes=(r_route,))
            p.emit("vector", lambda e: e.tensor_tensor(
                gscr[:, 32:40], gscr[:, 0:8], gscr[:, 8:16], op=ALU.min),
                reads=(r_route,), writes=(r_route,))
            p.emit("vector", lambda e: e.tensor_tensor(
                gscr[:, 40:48], gscr[:, 16:24], gscr[:, 24:32], op=ALU.max),
                reads=(r_route,), writes=(r_route,))
            p.emit("vector", lambda e: e.tensor_tensor(
                m2b, gscr[:, 32:40], gscr[:, 40:48], op=ALU.max),
                reads=(r_route,), writes=(r_route,))
            for gidx in range(4):
                p.emit("vector", lambda e, gidx=gidx: e.tensor_tensor(
                    gscr[:, 8 * gidx:8 * gidx + 8],
                    gsb[:, 8 * gidx:8 * gidx + 8], m2b, op=ALU.is_ge),
                    reads=(r_route,), writes=(r_route,))
            for ei in range(E):
                p.emit("vector", lambda e, ei=ei: e.tensor_tensor(
                    r2[:, 8 * ei:8 * ei + 8], r1[:, 8 * ei:8 * ei + 8],
                    gscr[:, 8 * (ei // 4):8 * (ei // 4) + 8], op=ALU.mult),
                    reads=(r_route,), writes=(r_route,))
            p.emit("vector", lambda e: e.tensor_tensor(
                r3, r2, eps16_b, op=ALU.add),
                reads=(r_route, r_const), writes=(r_route,))
            kview = r3.rearrange("p (e t) -> p e t", t=NT)
            mview = r1.rearrange("p (e t) -> p e t", t=NT)
            for t in range(NT):
                p.emit("vector", lambda e, t=t: e.max(out8, kview[:, :, t]),
                       reads=(r_route,), writes=(r_route,))
                p.emit("vector", lambda e, t=t: e.tensor_scalar(
                    mview[:, :, t], kview[:, :, t], out8[:, 3:4], None,
                    op0=ALU.is_ge),
                    reads=(r_route,), writes=(r_route,))
            p.emit("vector", lambda e: e.tensor_tensor(
                r2, r1, scores, op=ALU.mult),
                reads=(r_route, r_scores), writes=(r_route,))
            msview = r2.rearrange("p (e t) -> p e t", t=NT)
            for t in range(NT):
                p.emit("vector", lambda e, t=t: e.reduce_sum(
                    rsb[:, t:t + 1], msview[:, :, t], axis=AX.X),
                    reads=(r_route,), writes=(r_route,))
            p.emit("vector", lambda e: e.tensor_scalar(
                rsb, rsb, 1e-20, None, op0=ALU.add),
                reads=(r_route,), writes=(r_route,))
            p.emit("vector", lambda e: e.reciprocal(rrec, rsb),
                   reads=(r_route,), writes=(r_route,))
            cview = comb[:, 128 * par:128 * par + 128].rearrange(
                "p (e t) -> p e t", t=NT)
            for t in range(NT):
                p.emit("vector", lambda e, t=t: e.tensor_scalar(
                    cview[:, :, t], msview[:, :, t], rrec[:, t:t + 1], 2.5,
                    op0=ALU.mult, op1=ALU.mult),
                    reads=(r_route,), writes=(r_comb2[par],))
        sink.append(f_routing)

    def emit_xn_tiles(qq, src, sink=None):
        par_q = qq % 2
        lnb = 48 * par_q
        lnr = r_ln2[par_q]
        for t in range(NT):
            def f_xn(t=t):
                par = t % 2
                sap, rds = src(t)
                xnn = xn_nat[:, par * D:(par + 1) * D]
                p.emit("scalar",
                       lambda e, sap=sap, xnn=xnn, t=t: e.activation(
                           xnn, sap, AF.Identity,
                           bias=lnscr[:, lnb + 24 + t:lnb + 25 + t],
                           scale=lnscr[:, lnb + 16 + t:lnb + 17 + t]),
                       reads=rds + (lnr,), writes=(r_xnn[par],))
                for k in range(8):
                    p.emit("tensor",
                           lambda e, xnn=xnn, k=k: e.transpose(
                               PB[:, 128 * k:128 * k + 128],
                               xnn[:, 128 * k:128 * k + 128], identb),
                           reads=(r_xnn[par], r_const), writes=(RP(PB),))
                dst = xnT.rearrange("p (k t2) -> p k t2", k=8)[
                    :, :, 128 * t:128 * t + 128]
                p.emit("vector", lambda e, dst=dst: e.tensor_copy(
                    dst, PB.rearrange("p (k t2) -> p k t2", k=8)),
                    reads=(RP(PB),), writes=(r_xnT,))
            if sink is None:
                f_xn()
            else:
                sink.append(f_xn)

    def src_acc(t):
        return acc[:, t * D:(t + 1) * D], (r_acc[t],)

    def src_stage(t):
        return (big4f[:, 1024 * t:1024 * t + 1024],
                (r_big[2 * t], r_big[2 * t + 1]))

    for q in range(NQ):
        for t in range(NT):
            g = q * NT + t
            a_t = acc[:, t * D:(t + 1) * D]
            p.emit("sync",
                   lambda e, a_t=a_t, g=g: e.dma_start(out=a_t, in_=xv[g]),
                   writes=(r_acc[t],))
        if DO_ATT:
            p.emit("gpsimd", lambda e: e.dma_start(
                out=warea[:, 0:8192].rearrange("p (k j) -> p k j", k=8),
                in_=wqkv[:, 0:1024].rearrange("(k p) j -> p k j", p=128)),
                writes=(r_wblk[0], r_wblk[1]))
            p.emit("gpsimd", lambda e: e.dma_start(
                out=warea[:, 8192:16384].rearrange("p (k j) -> p k j", k=8),
                in_=wqkv[:, 1024:2048].rearrange("(k p) j -> p k j", p=128)),
                writes=(r_wblk[2], r_wblk[3]))
            p.emit("gpsimd", lambda e: e.dma_start(
                out=warea[:, 16384:24576].rearrange("p (k j) -> p k j", k=8),
                in_=wqkv[:, 2048:3072].rearrange("(k p) j -> p k j", p=128)),
                writes=(r_wblk[4], r_wblk[5]))

        if q == 0:
            inline = []
            emit_stats_router(0, src_acc, inline)
            for f in inline:
                f()
            emit_xn_tiles(0, src_acc)

        if DO_ATT:
            bank_i = 0
            for jt in range(16):
                wr = ((r_wblk[0], r_wblk[1]) if jt < 8
                      else (r_wblk[2], r_wblk[3]))
                for c in range(2):
                    bank = gu_banks[bank_i % 4]
                    bank_i += 1
                    mm_group(bank, [(
                        warea[:, (0 if jt < 8 else 8192) + 1024 * k + 128 * (jt % 8):
                              (0 if jt < 8 else 8192) + 1024 * k + 128 * (jt % 8) + 128],
                        xnT[:, TQ * k + 512 * c:TQ * k + 512 * c + 512])
                        for k in range(8)],
                        reads=wr + (r_xnT,))
                    dst = big4[:, TQ * jt + 512 * c:TQ * jt + 512 * c + 512]
                    p.emit("vector", lambda e, dst=dst, bank=bank:
                           e.tensor_copy(dst, bank),
                           reads=(RP(bank),), writes=(r_big[jt],))
            for t in range(NT):
                pr_blk = t // 2
                p0 = 64 * (t % 2)
                for dv in range(2):
                    bank = gu_banks[bank_i % 4]
                    bank_i += 1
                    base_t = 128 * t
                    mm_group(bank, [(
                        xnT[:, TQ * k + base_t:TQ * k + base_t + 64:2],
                        warea[:, 16384 + 1024 * k + 512 * dv:
                              16384 + 1024 * k + 512 * dv + 512])
                        for k in range(8)],
                        reads=(r_wblk[4], r_wblk[5], r_xnT),
                        out_ap=bank[0:32, :])
                    mm_group(bank, [(
                        xnT[:, TQ * k + base_t + 65:TQ * k + base_t + 128:2],
                        warea[:, 16384 + 1024 * k + 512 * dv:
                              16384 + 1024 * k + 512 * dv + 512])
                        for k in range(8)],
                        reads=(r_wblk[4], r_wblk[5], r_xnT),
                        out_ap=bank[32:64, :], tile_position=(0, 32))
                    dstv = v_sb[p0:p0 + 64,
                                TQ * pr_blk + 512 * dv:
                                TQ * pr_blk + 512 * dv + 512]
                    p.emit("vector",
                           lambda e, dstv=dstv, bank=bank: e.tensor_copy(
                               dstv, bank[0:64, :]),
                           reads=(RP(bank),), writes=(r_vsb[pr_blk],))

        mainline = []

        def ML(est, fn):
            mainline.append((est, fn))

        ob_rot = [PS, PD1]

        def emit_gu_item(lhs_g_fn, lhs_u_fn, rd_g, rd_u, dsth, hb_res,
                         gbank, ubank, spar):
            def fn():
                mm_group(gbank, lhs_g_fn(), reads=rd_g)
                mm_group(ubank, lhs_u_fn(), reads=rd_u)
                sil = sil_sb[:, 512 * spar:512 * spar + 512]
                p.emit("scalar", lambda e, sil=sil: e.activation(
                    sil, gbank, AF.Silu),
                    reads=(RP(gbank),), writes=(r_sil[spar],))
                p.emit("vector", lambda e, sil=sil: e.tensor_tensor(
                    dsth, sil, ubank, op=ALU.mult),
                    reads=(r_sil[spar], RP(ubank)), writes=(hb_res,))
            ML(3600, fn)

        def emit_down_item(bank, lhsrhs, reads, t, evac):
            def fn():
                mm_group(bank, lhsrhs, reads=reads)
                evac(bank)
            ML(len(lhsrhs) * 230 + 200, fn)

        if DO_SH:
            for half in range(2):
                b0 = 3 * half
                sgo, suo, sdo = 4096 * b0, 4096 * (b0 + 1), 4096 * (b0 + 2)

                def fn_dma(half=half, sgo=sgo, suo=suo, sdo=sdo, b0=b0):
                    p.emit("gpsimd", lambda e: e.dma_start(
                        out=warea[:, sgo:sgo + 4096].rearrange(
                            "p (k j) -> p k j", k=8),
                        in_=shgT[:, 512 * half:512 * half + 512].rearrange(
                            "(k p) j -> p k j", p=128)),
                        writes=(r_wblk[b0],))
                    p.emit("gpsimd", lambda e: e.dma_start(
                        out=warea[:, suo:suo + 4096].rearrange(
                            "p (k j) -> p k j", k=8),
                        in_=shuT[:, 512 * half:512 * half + 512].rearrange(
                            "(k p) j -> p k j", p=128)),
                        writes=(r_wblk[b0 + 1],))
                    p.emit("gpsimd", lambda e: e.dma_start(
                        out=warea[:, sdo:sdo + 4096].rearrange(
                            "p (k j) -> p k j", k=4),
                        in_=shdT[512 * half:512 * half + 512, :].rearrange(
                            "(k p) j -> p k j", p=128)),
                        writes=(r_wblk[b0 + 2],))
                ML(0, fn_dma)
                gu_i = 0
                for icc in range(4):
                    for c in range(2):
                        gbank = gu_banks[gu_i % 4]
                        ubank = gu_banks[(gu_i + 1) % 4]
                        gu_i += 2
                        spar = (2 * icc + c) % 2
                        dsth = hbuf[:, 1024 * icc + 512 * c:
                                    1024 * icc + 512 * c + 512]
                        emit_gu_item(
                            lambda icc=icc, c=c, sgo=sgo: [(
                                warea[:, sgo + 512 * k + 128 * icc:
                                      sgo + 512 * k + 128 * icc + 128],
                                xnT[:, TQ * k + 512 * c:TQ * k + 512 * c + 512])
                                for k in range(8)],
                            lambda icc=icc, c=c, suo=suo: [(
                                warea[:, suo + 512 * k + 128 * icc:
                                      suo + 512 * k + 128 * icc + 128],
                                xnT[:, TQ * k + 512 * c:TQ * k + 512 * c + 512])
                                for k in range(8)],
                            (r_wblk[b0], r_xnT), (r_wblk[b0 + 1], r_xnT),
                            dsth, r_hb[icc], gbank, ubank, spar)
                for t in range(NT):
                    for dd in range(2):
                        bank = ob_rot[dd % 2]

                        def evac(bank, t=t, dd=dd):
                            a_sl = acc[:, D * t + 512 * dd:
                                       D * t + 512 * dd + 512]
                            p.emit("vector", lambda e, a_sl=a_sl, bank=bank:
                                   e.tensor_tensor(a_sl, a_sl, bank,
                                                   op=ALU.add),
                                   reads=(RP(bank),), writes=(r_acc[t],))
                        emit_down_item(
                            bank,
                            [(hbuf[:, 1024 * icc2 + 128 * t:
                                   1024 * icc2 + 128 * t + 128],
                              warea[:, sdo + 1024 * icc2 + 512 * dd:
                                    sdo + 1024 * icc2 + 512 * dd + 512])
                             for icc2 in range(4)],
                            (r_hb[0], r_hb[1], r_hb[2], r_hb[3],
                             r_wblk[b0 + 2]),
                            t, evac)

        if DO_MOE:
            def emit_exp_dma_ml(ei):
                slot = ei % 2
                sofs = 12288 * slot
                sblks = (r_wblk[3 * slot], r_wblk[3 * slot + 1],
                         r_wblk[3 * slot + 2])

                def fn(sofs=sofs, ei=ei, sblks=sblks):
                    p.emit("gpsimd", lambda e: e.dma_start(
                        out=warea[:, sofs:sofs + 4096].rearrange(
                            "p (k i) -> p k i", k=8),
                        in_=gwT[ei].rearrange("(k p) i -> p k i", p=128)),
                        writes=(sblks[0],))
                    p.emit("gpsimd", lambda e: e.dma_start(
                        out=warea[:, sofs + 4096:sofs + 8192].rearrange(
                            "p (k i) -> p k i", k=8),
                        in_=uwT[ei].rearrange("(k p) i -> p k i", p=128)),
                        writes=(sblks[1],))
                    p.emit("gpsimd", lambda e: e.dma_start(
                        out=warea[:, sofs + 8192:sofs + 12288].rearrange(
                            "p (k h) -> p k h", k=4),
                        in_=dwT[ei].rearrange("(k p) h -> p k h", p=128)),
                        writes=(sblks[2],))
                ML(0, fn)

            emit_exp_dma_ml(0)
            emit_exp_dma_ml(1)
            cbv = comb[:, 128 * (q % 2):128 * (q % 2) + 128].rearrange(
                "p (e2 t) -> p e2 t", t=NT)
            for ei in range(E):
                slot = ei % 2
                sofs = 12288 * slot
                sblks = (r_wblk[3 * slot], r_wblk[3 * slot + 1],
                         r_wblk[3 * slot + 2])
                gu_i = 0
                for ic in range(4):
                    for c in range(2):
                        gbank = gu_banks[gu_i % 4]
                        ubank = gu_banks[(gu_i + 1) % 4]
                        gu_i += 2
                        spar = (2 * ic + c) % 2
                        dsth = hbuf[:, 1024 * ic + 512 * c:
                                    1024 * ic + 512 * c + 512]
                        emit_gu_item(
                            lambda ic=ic, c=c, sofs=sofs: [(
                                warea[:, sofs + 512 * k + 128 * ic:
                                      sofs + 512 * k + 128 * ic + 128],
                                xnT[:, TQ * k + 512 * c:TQ * k + 512 * c + 512])
                                for k in range(8)],
                            lambda ic=ic, c=c, sofs=sofs: [(
                                warea[:, sofs + 4096 + 512 * k + 128 * ic:
                                      sofs + 4096 + 512 * k + 128 * ic + 128],
                                xnT[:, TQ * k + 512 * c:TQ * k + 512 * c + 512])
                                for k in range(8)],
                            (sblks[0], r_xnT), (sblks[1], r_xnT),
                            dsth, r_hb[ic], gbank, ubank, spar)
                rot = ob_rot if ei < 4 else [PS, PD1, PAV]
                for t in range(NT):
                    for dd in range(2):
                        bank = rot[(2 * t + dd) % len(rot)]

                        def evac(bank, t=t, dd=dd, ei=ei, cbv=cbv, q=q):
                            a_sl = acc[:, D * t + 512 * dd:
                                       D * t + 512 * dd + 512]
                            p.emit("vector",
                                   lambda e, a_sl=a_sl, bank=bank, ei=ei,
                                   t=t, cbv=cbv:
                                   e.scalar_tensor_tensor(
                                       a_sl, bank, cbv[:, ei, t:t + 1], a_sl,
                                       op0=ALU.mult, op1=ALU.add),
                                   reads=(RP(bank), r_comb2[q % 2]),
                                   writes=(r_acc[t],))
                        emit_down_item(
                            bank,
                            [(hbuf[:, 1024 * ic2 + 128 * t:
                                   1024 * ic2 + 128 * t + 128],
                              warea[:, sofs + 8192 + 1024 * ic2 + 512 * dd:
                                    sofs + 8192 + 1024 * ic2 + 512 * dd + 512])
                             for ic2 in range(4)],
                            (r_hb[0], r_hb[1], r_hb[2], r_hb[3], sblks[2]),
                            t, evac)
                if ei + 2 < E:
                    emit_exp_dma_ml(ei + 2)

        att_steps = []

        if DO_ATT and ATT_LVL >= 3:
            for h in range(NH):
                hp = h // 2
                hpar = h % 2
                spar = h % 2
                qjt = hp
                kjt = 8 + hp
                q_r0 = 64 * hpar

                def fn_scores(h=h, hp=hp, hpar=hpar, spar=spar, qjt=qjt,
                              kjt=kjt, q_r0=q_r0):
                    for n in range(NT):
                        lhs = big4[q_r0:q_r0 + 64,
                                   TQ * qjt + 128 * n:TQ * qjt + 128 * n + 128]
                        kbase = TQ * kjt + 128 * n
                        rhs_a = big4[q_r0:q_r0 + 64, kbase:kbase + 64:2]
                        rhs_b = big4[q_r0:q_r0 + 64,
                                     kbase + 65:kbase + 128:2]
                        p.emit("tensor",
                               lambda e, lhs=lhs, rhs_a=rhs_a, n=n: e.matmul(
                                   PR[:, 64 * n:64 * n + 32], lhs, rhs_a,
                                   start=True, stop=True),
                               reads=(r_big[qjt], r_big[kjt]),
                               writes=(RP(PR),))
                        p.emit("tensor",
                               lambda e, lhs=lhs, rhs_b=rhs_b, n=n: e.matmul(
                                   PR[:, 64 * n + 32:64 * n + 64], lhs,
                                   rhs_b, start=True, stop=True),
                               reads=(r_big[qjt], r_big[kjt]),
                               writes=(RP(PR),))
                    smb = sm8[:, 32 * spar:32 * spar + 32]
                    m1, nb1 = smb[:, 0:1], smb[:, 1:2]
                    s8, r8 = smb[:, 16:24], smb[:, 24:32]
                    p.emit("vector", lambda e, m1=m1: e.reduce_max(
                        m1, PR, axis=AX.X),
                        reads=(RP(PR),), writes=(r_sm8[spar],))
                    p.emit("vector", lambda e, m1=m1, nb1=nb1: e.tensor_scalar(
                        nb1, m1, -0.125, None, op0=ALU.mult),
                        reads=(r_sm8[spar],), writes=(r_sm8[spar],))
                    expd = exp_sb[:, 512 * spar:512 * spar + 512]
                    p.emit("scalar", lambda e, expd=expd, nb1=nb1:
                           e.activation(expd, PR, AF.Exp,
                                        bias=nb1, scale=0.125),
                           reads=(RP(PR), r_sm8[spar]),
                           writes=(r_exps[spar],))
                    e3 = expd.rearrange("p (n k) -> p n k", n=8)
                    p.emit("vector", lambda e, e3=e3, s8=s8: e.reduce_sum(
                        s8, e3, axis=AX.X),
                        reads=(r_exps[spar],), writes=(r_sm8[spar],))
                    p.emit("vector", lambda e, s8=s8, r8=r8: e.reciprocal(
                        r8, s8),
                        reads=(r_sm8[spar],), writes=(r_sm8[spar],))
                    attnd = attn_sb[:, 512 * spar:512 * spar + 512]
                    for n in range(NT):
                        p.emit("vector",
                               lambda e, n=n, attnd=attnd, expd=expd, r8=r8:
                               e.tensor_scalar(
                                   attnd[:, 64 * n:64 * n + 64],
                                   expd[:, 64 * n:64 * n + 64],
                                   r8[:, n:n + 1], None, op0=ALU.mult),
                               reads=(r_exps[spar], r_sm8[spar]),
                               writes=(r_attns[spar],))
                att_steps.append((5000, fn_scores))

                if ATT_LVL >= 4:
                    def fn_tr(h=h, spar=spar):
                        attnd = attn_sb[:, 512 * spar:512 * spar + 512]
                        for pr2 in range(4):
                            p.emit("tensor",
                                   lambda e, pr2=pr2, attnd=attnd:
                                   e.transpose(
                                       PB[:, 128 * pr2:128 * pr2 + 128],
                                       attnd[:, 128 * pr2:128 * pr2 + 128],
                                       identb),
                                   reads=(r_attns[spar], r_const),
                                   writes=(RP(PB),))
                        attnTd = attnT_sb[:, 512 * spar:512 * spar + 512]
                        p.emit("vector", lambda e, attnTd=attnTd:
                               e.tensor_copy(attnTd, PB[:, 0:512]),
                               reads=(RP(PB),), writes=(r_attnT[spar],))
                    att_steps.append((1200, fn_tr))

                if ATT_LVL >= 5:
                    def fn_av(h=h, hp=hp, hpar=hpar, spar=spar):
                        attnTd = attnT_sb[:, 512 * spar:512 * spar + 512]
                        for n in range(NT):
                            prb = n // 2
                            vr0 = 64 * (n % 2)
                            av_bank = PR if n % 2 == 0 else PAV
                            lhsv = v_sb[vr0:vr0 + 64,
                                        TQ * prb + 64 * h:
                                        TQ * prb + 64 * h + 64]
                            rhsa = attnTd[vr0:vr0 + 64,
                                          128 * prb:128 * prb + 128]
                            p.emit("tensor",
                                   lambda e, lhsv=lhsv, rhsa=rhsa, n=n,
                                   hpar=hpar, av_bank=av_bank: e.matmul(
                                       av_bank[64 * hpar:64 * hpar + 64,
                                               128 * (n // 2):
                                               128 * (n // 2) + 128],
                                       lhsv, rhsa, start=True, stop=True,
                                       tile_position=(64 * (n % 2),
                                                      64 * hpar)),
                                   reads=(r_vsb[prb], r_attnT[spar]),
                                   writes=(RP(av_bank),))
                        r0 = 64 * hpar
                        for par2 in range(2):
                            av_bank = PR if par2 == 0 else PAV
                            base = TQ * hp + 128 * par2
                            dsta = _four_block_cols(aoT, base, 256, 128)
                            p.emit("vector",
                                   lambda e, dsta=dsta, av_bank=av_bank,
                                   r0=r0: e.tensor_copy(
                                       dsta[r0:r0 + 64],
                                       av_bank[r0:r0 + 64, 0:512]),
                                   reads=(RP(av_bank),),
                                   writes=(r_aoT[hp],))
                    att_steps.append((700, fn_av))

        pa_items = []
        if q + 1 < NQ:
            for t in range(NT):
                g2 = (q + 1) * NT + t

                def f_stage(t=t, g2=g2):
                    p.emit("sync", lambda e, t=t, g2=g2: e.dma_start(
                        out=big4f[:, 1024 * t:1024 * t + 1024], in_=xv[g2]),
                        writes=(r_big[2 * t], r_big[2 * t + 1]))
                pa_items.append(f_stage)
            emit_stats_router(q + 1, src_stage, pa_items)

        ml_pos = [0]

        def pump(ns):
            acc_ns = 0
            while ml_pos[0] < len(mainline) and acc_ns < ns:
                est, fn = mainline[ml_pos[0]]
                ml_pos[0] += 1
                fn()
                acc_ns += est

        for i, (fill, fn) in enumerate(att_steps):
            fn()
            pump(fill if i >= 4 else 2000)
        for f in pa_items:
            f()
            pump(6000)
        while ml_pos[0] < len(mainline):
            mainline[ml_pos[0]][1]()
            ml_pos[0] += 1

        ay_items = []
        if DO_ATT and ATT_LVL >= 6:
            p.emit("gpsimd", lambda e: e.dma_start(
                out=warea[:, 8192:16384].rearrange("p (k j) -> p k j", k=8),
                in_=wout.rearrange("(k p) j -> p k j", p=128)),
                writes=(r_wblk[2], r_wblk[3]))
            for t in range(NT):
                for dd in range(2):
                    bank = ob_rot[dd % 2]

                    def f_ay(t=t, dd=dd, bank=bank):
                        mm_group(bank, [(
                            aoT[:, TQ * jt2 + 128 * t:
                                TQ * jt2 + 128 * t + 128],
                            warea[:, 8192 + 1024 * jt2 + 512 * dd:
                                  8192 + 1024 * jt2 + 512 * dd + 512])
                            for jt2 in range(8)],
                            reads=(r_aoT[0], r_aoT[1], r_aoT[2], r_aoT[3],
                                   r_aoT[4], r_aoT[5], r_aoT[6], r_aoT[7],
                                   r_wblk[2], r_wblk[3]))
                        a_sl = acc[:, D * t + 512 * dd:
                                   D * t + 512 * dd + 512]
                        p.emit("vector", lambda e, a_sl=a_sl, bank=bank:
                               e.tensor_tensor(a_sl, a_sl, bank, op=ALU.add),
                               reads=(RP(bank),), writes=(r_acc[t],))
                    ay_items.append(f_ay)
        xn_items = []
        if q + 1 < NQ:
            emit_xn_tiles(q + 1, src_stage, sink=xn_items)
        tails = [ay_items, xn_items]
        while any(tails):
            for tl in tails:
                if tl:
                    tl.pop(0)()

        p.emit("sync", lambda e, q=q: e.dma_start(
            out=outv[q], in_=acc.rearrange("p (t d) -> p t d", t=NT)),
            reads=tuple(r_acc), writes=(r_out,))

    p.emit("vector", lambda e: e.nop(), reads=(r_out,))

    import contextlib
    with contextlib.ExitStack() as stack:
        sems = {}
        for e in ENGINES:
            sems[e] = stack.enter_context(nc.semaphore(name=f"s_{e}"))
        block = stack.enter_context(nc.Block())
        p.finalize(block, sems)
    return nc


def _two_block_cols(buf, base, stride, width):
    v = buf[:, base:base + stride + 1:stride]
    w = buf[:, base:base + width]
    u = v.copy()
    u.ap = v.ap + w.ap[1:]
    return u


def _four_block_cols(buf, base, stride, width):
    v = buf[:, base:base + 3 * stride + 1:stride]
    w = buf[:, base:base + width]
    u = v.copy()
    u.ap = v.ap + w.ap[1:]
    return u


def _ksub_cols(buf, base):
    v = buf[:, base:base + 66:65]
    w = buf[:, base:base + 64:2]
    u = v.copy()
    u.ap = v.ap + w.ap[1:]
    return u


def _ksub_ap(big4, r0, kbase):
    v = big4[r0:r0 + 64, kbase:kbase + 66:65]
    w = big4[r0:r0 + 64, kbase:kbase + 64:2]
    u = v.copy()
    u.ap = v.ap + w.ap[1:]
    return u


_NC_CACHE = None


def kernel(L0_attn_x, L0_attn_W_qkv, L0_attn_W_out, L0_ffn_router_weight,
           L0_ffn_e_bias, L0_ffn_gate_w, L0_ffn_up_w, L0_ffn_down_w,
           L0_ffn_sh_gate_w, L0_ffn_sh_up_w, L0_ffn_sh_down_w):
    global _NC_CACHE
    f32 = np.float32

    def A(v):
        return np.ascontiguousarray(np.asarray(v, dtype=f32))

    x = A(L0_attn_x)
    wqkv = A(L0_attn_W_qkv)
    wout = A(L0_attn_W_out)
    rwT = A(np.asarray(L0_ffn_router_weight, f32).T)
    eb = np.asarray(L0_ffn_e_bias, f32)
    ebias_rep = A(np.repeat(eb, NT)[None, :])
    eps16_rep = A(np.repeat((15 - np.arange(E)) * 1e-30, NT)[None, :])
    epsg_rep = A(np.repeat((3 - np.arange(4)) * 1e-30, NT)[None, :])
    gwT = A(np.asarray(L0_ffn_gate_w, f32).transpose(0, 2, 1))
    uwT = A(np.asarray(L0_ffn_up_w, f32).transpose(0, 2, 1))
    dwT = A(np.asarray(L0_ffn_down_w, f32).transpose(0, 2, 1))
    shgT = A(np.asarray(L0_ffn_sh_gate_w, f32).T)
    shuT = A(np.asarray(L0_ffn_sh_up_w, f32).T)
    shdT = A(np.asarray(L0_ffn_sh_down_w, f32).T)
    identf = np.eye(128, dtype=f32)

    if _NC_CACHE is None:
        _NC_CACHE = build_program()
    nc = _NC_CACHE

    in_maps = []
    for c in range(N_CORES):
        xc = np.ascontiguousarray(x[c])
        xTt = np.ascontiguousarray(
            xc.reshape(T // 128, 128, D).transpose(0, 2, 1))
        in_maps.append({
            "x": xc, "xTt": xTt, "wqkv": wqkv, "wout": wout, "rwT": rwT,
            "ebias_rep": ebias_rep, "eps16_rep": eps16_rep,
            "epsg_rep": epsg_rep, "gwT": gwT, "uwT": uwT, "dwT": dwT,
            "shgT": shgT, "shuT": shuT, "shdT": shdT, "identf": identf,
        })
    res = run_bass_kernel_spmd(nc, in_maps, core_ids=list(range(N_CORES)))
    return np.stack([r["out"] for r in res.results], axis=0)


# version 13 (final)
# speedup vs baseline: 1.0285x; 1.0275x over previous
import numpy as np
import concourse.bass as bass
import concourse.mybir as mybir
from concourse.bass_utils import run_bass_kernel_spmd

F32 = mybir.dt.float32
BF16 = mybir.dt.bfloat16
AF = mybir.ActivationFunctionType
ALU = mybir.AluOpType
AX = mybir.AxisListType

N_CORES = 8
T = 4096
D = 1024
NQ = 4
TQ = T // NQ
NT = TQ // 128
E = 16
IM = 512
NH = 16
DH = 64

import os as _os
ATT_LVL = int(_os.environ.get("K_ATT", "9"))
DO_ATT = ATT_LVL > 0
DO_SH = _os.environ.get("K_SH", "1") == "1"
DO_MOE = _os.environ.get("K_MOE", "1") == "1"

NDQ = 4
SYNC_QS = [f"sync{i}" for i in range(NDQ)]
GPS_QS = [f"gpsimd{i}" for i in range(NDQ)]
ENGINES = ["tensor", "vector", "scalar"] + SYNC_QS + GPS_QS
DMA_ENGINES = set(SYNC_QS) | set(GPS_QS)
REAL_ENGINE = {e: e for e in ["tensor", "vector", "scalar"]}
REAL_ENGINE.update({q: "sync" for q in SYNC_QS})
REAL_ENGINE.update({q: "gpsimd" for q in GPS_QS})


class Op:
    __slots__ = ("eng", "fn", "deps", "inc_step", "need_inc", "val", "idx")

    def __init__(self, eng, fn, idx):
        self.eng = eng
        self.fn = fn
        self.idx = idx
        self.deps = set()
        self.inc_step = 16 if eng in DMA_ENGINES else 1
        self.need_inc = False
        self.val = None


class Res:
    __slots__ = ("name", "last_write", "readers")

    def __init__(self, name):
        self.name = name
        self.last_write = None
        self.readers = []


class Prog:
    def __init__(self):
        self.ops = []
        self.rr = {"sync": 0, "gpsimd": 0}
        self.qlast = {q: None for q in DMA_ENGINES}

    def emit(self, eng, fn, reads=(), writes=()):
        if eng in ("sync", "gpsimd"):
            qi = self.rr[eng]
            self.rr[eng] = (qi + 1) % NDQ
            eng = f"{eng}{qi}"
        op = Op(eng, fn, len(self.ops))
        for r in reads:
            if r.last_write is not None:
                op.deps.add(r.last_write)
        for w in writes:
            if w.last_write is not None:
                op.deps.add(w.last_write)
            for rd in w.readers:
                op.deps.add(rd)
        if eng in DMA_ENGINES:
            if self.qlast[eng] is not None:
                op.deps.add(self.qlast[eng])
            self.qlast[eng] = op
        op.deps = {d for d in op.deps if d.eng != eng or eng != "tensor"}
        for w in writes:
            w.last_write = op
            w.readers = []
        for r in reads:
            r.readers.append(op)
        self.ops.append(op)
        return op

    def finalize(self, block, sems):
        for op in self.ops:
            if op.eng in DMA_ENGINES:
                op.need_inc = True
            for d in op.deps:
                d.need_inc = True
        counters = {e: 0 for e in ENGINES}
        for op in self.ops:
            if op.need_inc:
                counters[op.eng] += op.inc_step
                op.val = counters[op.eng]
        for real in ["tensor", "vector", "scalar", "sync", "gpsimd"]:
            eng_ops = [o for o in self.ops if REAL_ENGINE[o.eng] == real]

            def body(eng, eng_ops=eng_ops):
                waited = {}
                for op in eng_ops:
                    for d in sorted(op.deps, key=lambda o: o.idx):
                        if d.val is None:
                            continue
                        if waited.get(d.eng, -1) >= d.val:
                            continue
                        eng.wait_ge(sems[d.eng], d.val)
                        waited[d.eng] = d.val
                    inst = op.fn(eng)
                    if op.need_inc:
                        inst.then_inc(sems[op.eng], op.inc_step)

            getattr(block, real)(body)


def build_program():
    nc = bass.Bass("TRN2", target_bir_lowering=False)

    def din(name, shape):
        return nc.dram_tensor(name, shape, F32, kind="ExternalInput").ap()

    x = din("x", [T, D])
    xTt = din("xTt", [T // 128, D, 128])
    wqkv = din("wqkv", [D, 3 * D])
    wout = din("wout", [D, D])
    rwT = din("rwT", [D, E])
    ebias_rep = din("ebias_rep", [1, E * NT])
    eps16_rep = din("eps16_rep", [1, E * NT])
    epsg_rep = din("epsg_rep", [1, 4 * NT])
    gwT = din("gwT", [E, D, IM])
    uwT = din("uwT", [E, D, IM])
    dwT = din("dwT", [E, IM, D])
    shgT = din("shgT", [D, D])
    shuT = din("shuT", [D, D])
    shdT = din("shdT", [D, D])
    identf = din("identf", [128, 128])
    out = nc.dram_tensor("out", [T, D], F32, kind="ExternalOutput").ap()
    DBG = _os.environ.get("K_DBG", "0") == "1"
    if DBG:
        dbg_xnT = nc.dram_tensor("dbg_xnT", [128, 8 * TQ], BF16,
                                 kind="ExternalOutput").ap()
        dbg_r = nc.dram_tensor("dbg_r", [128, 3 * E * NT], F32,
                               kind="ExternalOutput").ap()

    def sb(name, shape, dt=F32):
        return nc.alloc_sbuf_tensor(name, shape, dt).ap()

    acc = sb("acc", [128, NT * D])
    xnT = sb("xnT", [128, 8 * TQ], BF16)
    warea = sb("warea", [128, 6 * 4096], BF16)
    big4 = sb("big4", [128, 16 * TQ], BF16)
    aoT = sb("aoT", [128, 8 * TQ], BF16)
    v_sb = sb("v_sb", [128, 4 * TQ], BF16)
    hbuf = sb("hbuf", [128, 4 * TQ], BF16)
    xn_nat = sb("xn_nat", [128, 2 * D], BF16)
    xt_sb = sb("xt_sb", [128, 2 * D])
    exp_sb = sb("exp_sb", [128, 2 * 512])
    attn_sb = sb("attn_sb", [128, 2 * 512], BF16)
    attnT_sb = sb("attnT_sb", [128, 2 * 512], BF16)
    sil_sb = sb("sil_sb", [128, 2 * 512], BF16)
    stbuf = sb("stbuf", [128, 12 * NT])
    lnscr = sb("lnscr", [128, 96])
    zbuf = sb("zbuf", [128, E * NT])
    scores = sb("scores", [128, E * NT])
    r1 = sb("r1", [128, E * NT])
    r2 = sb("r2", [128, E * NT])
    r3 = sb("r3", [128, E * NT])
    comb = sb("comb", [128, 2 * E * NT])
    gsb = sb("gsb", [128, 4 * NT])
    gscr = sb("gscr", [128, 6 * NT])
    m2b = sb("m2b", [128, NT])
    out8 = sb("out8", [128, 8])
    rsb = sb("rsb", [128, NT])
    rrec = sb("rrec", [128, NT])
    rt16 = sb("rt16", [128, E])
    sm8 = sb("sm8", [128, 8 * NT])
    rwT_sb = sb("rwT_sb", [128, 8 * E])
    rwsum_b = sb("rwsum_b", [128, E])
    ebias_b = sb("ebias_b", [128, E * NT])
    eps16_b = sb("eps16_b", [128, E * NT])
    epsg_b = sb("epsg_b", [128, 4 * NT])
    identb = sb("identb", [128, 128], BF16)
    ones_r = sb("ones_r", [1, 128])
    ones_c = sb("ones_c", [128, 1])
    row_st = sb("row_st", [1, 4 * E * NT])

    PB = nc.alloc_psum_tensor("PB", [128, 1024], BF16).ap()
    PAV = nc.alloc_psum_tensor("PAV", [128, 512], F32).ap()
    PG = nc.alloc_psum_tensor("PG", [128, 512], F32).ap()
    PU = nc.alloc_psum_tensor("PU", [128, 512], F32).ap()
    PD0 = nc.alloc_psum_tensor("PD0", [128, 512], F32).ap()
    PD1 = nc.alloc_psum_tensor("PD1", [128, 512], F32).ap()
    PS = nc.alloc_psum_tensor("PS", [128, 512], F32).ap()
    PR = nc.alloc_psum_tensor("PR", [128, 512], F32).ap()

    p = Prog()

    r_acc = [Res(f"acc{t}") for t in range(NT)]
    r_xnT = Res("xnT")
    r_wblk = [Res(f"wblk{i}") for i in range(6)]
    r_big = [Res(f"big{i}") for i in range(16)]
    r_aoT = [Res(f"aoT{i}") for i in range(8)]
    r_vsb = [Res(f"vsb{i}") for i in range(4)]
    r_hb = [Res(f"hb{i}") for i in range(4)]
    r_xnn = [Res(f"xnn{i}") for i in range(2)]
    r_xts = [Res(f"xts{i}") for i in range(2)]
    r_exps = [Res(f"exps{i}") for i in range(4)]
    r_attns = [Res(f"attns{i}") for i in range(4)]
    r_attnT = [Res(f"attnT{i}") for i in range(4)]
    r_sil = [Res(f"sil{i}") for i in range(2)]
    r_st = Res("stbuf")
    r_ln2 = [Res("lnscr0"), Res("lnscr1")]
    r_z = Res("zbuf")
    r_scores = Res("scores")
    r_route = Res("route_scratch")
    r_comb2 = [Res("comb0"), Res("comb1")]
    r_rt16 = Res("rt16")
    r_sm8 = [Res(f"sm8{i}") for i in range(4)]
    r_out = Res("out_hbm")
    r_pb = {id(PB): Res("PB"), id(PAV): Res("PAV"), id(PG): Res("PG"),
            id(PU): Res("PU"), id(PD0): Res("PD0"), id(PD1): Res("PD1"),
            id(PS): Res("PS"), id(PR): Res("PR")}
    r_const = Res("consts")

    def RP(bank):
        return r_pb[id(bank)]

    def mm_group(bank, lhsrhs, reads, out_ap=None, tile_position=None):
        ob = bank if out_ap is None else out_ap
        n = len(lhsrhs)
        for i, (lh, rh) in enumerate(lhsrhs):
            p.emit("tensor",
                   lambda e, lh=lh, rh=rh, i=i, n=n, ob=ob: e.matmul(
                       ob, lh, rh, start=(i == 0), stop=(i == n - 1),
                       tile_position=tile_position),
                   reads=reads, writes=(RP(bank),))

    p.emit("sync", lambda e: e.dma_start(
        out=rwT_sb.rearrange("p (k e) -> p k e", k=8),
        in_=rwT.rearrange("(k p) e -> p k e", p=128)),
        writes=(r_const,))
    p.emit("gpsimd", lambda e: e.dma_start(out=identb, in_=identf),
           writes=(r_const,))
    p.emit("sync", lambda e: e.dma_start(
        out=row_st[:, 0:128], in_=ebias_rep), writes=(r_const,))
    p.emit("sync", lambda e: e.dma_start(
        out=row_st[:, 128:256], in_=eps16_rep), writes=(r_const,))
    p.emit("sync", lambda e: e.dma_start(
        out=row_st[:, 256:288], in_=epsg_rep), writes=(r_const,))
    p.emit("vector", lambda e: e.memset(ones_r, 1.0), writes=(r_const,))
    p.emit("vector", lambda e: e.memset(ones_c, 1.0), writes=(r_const,))
    mm_group(PR, [(ones_c, rwT_sb[:, 16 * k:16 * k + 16]) for k in range(8)],
             reads=(r_const,), out_ap=PR[0:1, 0:16])
    p.emit("vector", lambda e: e.tensor_copy(row_st[:, 288:304], PR[0:1, 0:16]),
           reads=(RP(PR),), writes=(r_const,))
    for (src, dst, n) in ((row_st[:, 288:304], rwsum_b, 16),
                          (row_st[:, 0:128], ebias_b, 128),
                          (row_st[:, 128:256], eps16_b, 128),
                          (row_st[:, 256:288], epsg_b, 32)):
        mm_group(PR, [(ones_r, src)], reads=(r_const,), out_ap=PR[:, 0:n])
        p.emit("vector",
               lambda e, dst=dst, n=n: e.tensor_copy(dst, PR[:, 0:n]),
               reads=(RP(PR),), writes=(r_const,))

    xv = x.rearrange("(g p) d -> g p d", p=128)
    outv = out.rearrange("(q t p) d -> q p t d", p=128, t=NT)

    gu_banks = [PG, PU, PD0, PD1]

    big4f = big4.bitcast(F32)
    zview = zbuf.rearrange("p (e t) -> p e t", t=NT)

    def emit_stats_router(qq, src, sink):
        par = qq % 2
        lnb = 48 * par
        lnr = r_ln2[par]

        for t in range(NT):
            def f_st(t=t):
                ap, rds = src(t)
                p.emit("vector", lambda e, ap=ap, t=t: e.bn_stats(
                    stbuf[:, 12 * t:12 * t + 6], ap[:, 0:512]),
                    reads=rds, writes=(r_st,))
                p.emit("vector", lambda e, ap=ap, t=t: e.bn_stats(
                    stbuf[:, 12 * t + 6:12 * t + 12], ap[:, 512:1024]),
                    reads=rds, writes=(r_st,))
            sink.append(f_st)

        def f_smalls():
            for t in range(NT):
                p.emit("vector", lambda e, t=t: e.bn_aggr(
                    lnscr[:, lnb + 2 * t:lnb + 2 * t + 2],
                    stbuf[:, 12 * t:12 * t + 12]),
                    reads=(r_st,), writes=(lnr,))
            mv = lnscr[:, lnb:lnb + 16].rearrange(
                "p (t two) -> p two t", two=2)
            mean8, var8 = mv[:, 0, :], mv[:, 1, :]
            p.emit("vector", lambda e: e.tensor_scalar(
                lnscr[:, lnb + 40:lnb + 48], var8, 1e-5, None, op0=ALU.add),
                reads=(lnr,), writes=(lnr,))
            p.emit("vector", lambda e: e.reciprocal(
                lnscr[:, lnb + 16:lnb + 24], lnscr[:, lnb + 40:lnb + 48]),
                reads=(lnr,), writes=(lnr,))
            p.emit("scalar", lambda e: e.activation(
                lnscr[:, lnb + 16:lnb + 24], lnscr[:, lnb + 16:lnb + 24],
                AF.Sqrt),
                reads=(lnr,), writes=(lnr,))
            p.emit("vector", lambda e: e.tensor_tensor(
                lnscr[:, lnb + 24:lnb + 32], mean8,
                lnscr[:, lnb + 16:lnb + 24], op=ALU.mult),
                reads=(lnr,), writes=(lnr,))
            p.emit("vector", lambda e: e.tensor_scalar(
                lnscr[:, lnb + 24:lnb + 32], lnscr[:, lnb + 24:lnb + 32],
                -1.0, None, op0=ALU.mult),
                reads=(lnr,), writes=(lnr,))
            p.emit("vector", lambda e: e.tensor_scalar(
                lnscr[:, lnb + 32:lnb + 40], mean8, -1.0, None, op0=ALU.mult),
                reads=(lnr,), writes=(lnr,))
        sink.append(f_smalls)

        for t in range(NT):
            def f_rt(t=t):
                g = qq * NT + t
                xpar = t % 2
                xts = xt_sb[:, xpar * D:(xpar + 1) * D]
                p.emit("sync", lambda e, xts=xts, g=g: e.dma_start(
                    out=xts.rearrange("p (k j) -> p k j", k=8),
                    in_=xTt[g].rearrange("(k p) j -> p k j", p=128)),
                    writes=(r_xts[xpar],))
                mm_group(PR, [(
                    xt_sb[:, xpar * D + 128 * k:xpar * D + 128 * k + 128],
                    rwT_sb[:, 16 * k:16 * k + 16]) for k in range(8)],
                    reads=(r_xts[xpar], r_const), out_ap=PR[:, 0:16])
                p.emit("vector", lambda e, t=t: e.scalar_tensor_tensor(
                    rt16, rwsum_b, lnscr[:, lnb + 32 + t:lnb + 33 + t],
                    PR[:, 0:16], op0=ALU.mult, op1=ALU.add),
                    reads=(RP(PR), r_const, lnr), writes=(r_rt16,))
                p.emit("vector", lambda e, t=t: e.tensor_scalar(
                    zview[:, :, t], rt16, lnscr[:, lnb + 16 + t:lnb + 17 + t],
                    None, op0=ALU.mult),
                    reads=(r_rt16, lnr), writes=(r_z,))
            sink.append(f_rt)

        def f_routing():
            p.emit("scalar", lambda e: e.activation(scores, zbuf, AF.Sigmoid),
                   reads=(r_z,), writes=(r_scores,))
            scv = r1.rearrange("p (g j t) -> p g j t", g=4, j=4)
            p.emit("vector", lambda e: e.tensor_tensor(
                r1, scores, ebias_b, op=ALU.add),
                reads=(r_scores, r_const), writes=(r_route,))
            pairs = [(0, 1), (0, 2), (0, 3), (1, 2), (1, 3), (2, 3)]
            for i, (ja, jb) in enumerate(pairs):
                dst = gsb if i == 0 else gscr[:, 0:32]
                p.emit("vector",
                       lambda e, ja=ja, jb=jb, dst=dst: e.tensor_tensor(
                           dst.rearrange("p (g t) -> p g t", g=4),
                           scv[:, :, ja, :], scv[:, :, jb, :], op=ALU.add),
                       reads=(r_route,), writes=(r_route,))
                if i > 0:
                    p.emit("vector", lambda e: e.tensor_tensor(
                        gsb, gsb, gscr[:, 0:32], op=ALU.max),
                        reads=(r_route,), writes=(r_route,))
            p.emit("vector", lambda e: e.tensor_tensor(
                gsb, gsb, epsg_b, op=ALU.add),
                reads=(r_route, r_const), writes=(r_route,))
            gv2 = gsb.rearrange("p (h two t) -> p h two t", h=2, two=2)
            p.emit("vector", lambda e: e.tensor_tensor(
                gscr[:, 0:16].rearrange("p (h t) -> p h t", h=2),
                gv2[:, :, 0, :], gv2[:, :, 1, :], op=ALU.max),
                reads=(r_route,), writes=(r_route,))
            p.emit("vector", lambda e: e.tensor_tensor(
                gscr[:, 16:32].rearrange("p (h t) -> p h t", h=2),
                gv2[:, :, 0, :], gv2[:, :, 1, :], op=ALU.min),
                reads=(r_route,), writes=(r_route,))
            p.emit("vector", lambda e: e.tensor_tensor(
                gscr[:, 32:40], gscr[:, 0:8], gscr[:, 8:16], op=ALU.min),
                reads=(r_route,), writes=(r_route,))
            p.emit("vector", lambda e: e.tensor_tensor(
                gscr[:, 40:48], gscr[:, 16:24], gscr[:, 24:32], op=ALU.max),
                reads=(r_route,), writes=(r_route,))
            p.emit("vector", lambda e: e.tensor_tensor(
                m2b, gscr[:, 32:40], gscr[:, 40:48], op=ALU.max),
                reads=(r_route,), writes=(r_route,))
            for gidx in range(4):
                p.emit("vector", lambda e, gidx=gidx: e.tensor_tensor(
                    gscr[:, 8 * gidx:8 * gidx + 8],
                    gsb[:, 8 * gidx:8 * gidx + 8], m2b, op=ALU.is_ge),
                    reads=(r_route,), writes=(r_route,))
            for ei in range(E):
                p.emit("vector", lambda e, ei=ei: e.tensor_tensor(
                    r2[:, 8 * ei:8 * ei + 8], r1[:, 8 * ei:8 * ei + 8],
                    gscr[:, 8 * (ei // 4):8 * (ei // 4) + 8], op=ALU.mult),
                    reads=(r_route,), writes=(r_route,))
            p.emit("vector", lambda e: e.tensor_tensor(
                r3, r2, eps16_b, op=ALU.add),
                reads=(r_route, r_const), writes=(r_route,))
            kview = r3.rearrange("p (e t) -> p e t", t=NT)
            mview = r1.rearrange("p (e t) -> p e t", t=NT)
            for t in range(NT):
                p.emit("vector", lambda e, t=t: e.max(out8, kview[:, :, t]),
                       reads=(r_route,), writes=(r_route,))
                p.emit("vector", lambda e, t=t: e.tensor_scalar(
                    mview[:, :, t], kview[:, :, t], out8[:, 3:4], None,
                    op0=ALU.is_ge),
                    reads=(r_route,), writes=(r_route,))
            p.emit("vector", lambda e: e.tensor_tensor(
                r2, r1, scores, op=ALU.mult),
                reads=(r_route, r_scores), writes=(r_route,))
            msview = r2.rearrange("p (e t) -> p e t", t=NT)
            for t in range(NT):
                p.emit("vector", lambda e, t=t: e.reduce_sum(
                    rsb[:, t:t + 1], msview[:, :, t], axis=AX.X),
                    reads=(r_route,), writes=(r_route,))
            p.emit("vector", lambda e: e.tensor_scalar(
                rsb, rsb, 1e-20, None, op0=ALU.add),
                reads=(r_route,), writes=(r_route,))
            p.emit("vector", lambda e: e.reciprocal(rrec, rsb),
                   reads=(r_route,), writes=(r_route,))
            cview = comb[:, 128 * par:128 * par + 128].rearrange(
                "p (e t) -> p e t", t=NT)
            for t in range(NT):
                p.emit("vector", lambda e, t=t: e.tensor_scalar(
                    cview[:, :, t], msview[:, :, t], rrec[:, t:t + 1], 2.5,
                    op0=ALU.mult, op1=ALU.mult),
                    reads=(r_route,), writes=(r_comb2[par],))
        sink.append(f_routing)

    def emit_xn_tiles(qq, src, sink=None):
        par_q = qq % 2
        lnb = 48 * par_q
        lnr = r_ln2[par_q]
        for t in range(NT):
            def f_xn(t=t):
                par = t % 2
                sap, rds = src(t)
                xnn = xn_nat[:, par * D:(par + 1) * D]
                p.emit("scalar",
                       lambda e, sap=sap, xnn=xnn, t=t: e.activation(
                           xnn, sap, AF.Identity,
                           bias=lnscr[:, lnb + 24 + t:lnb + 25 + t],
                           scale=lnscr[:, lnb + 16 + t:lnb + 17 + t]),
                       reads=rds + (lnr,), writes=(r_xnn[par],))
                for k in range(8):
                    p.emit("tensor",
                           lambda e, xnn=xnn, k=k: e.transpose(
                               PB[:, 128 * k:128 * k + 128],
                               xnn[:, 128 * k:128 * k + 128], identb),
                           reads=(r_xnn[par], r_const), writes=(RP(PB),))
                dst = xnT.rearrange("p (k t2) -> p k t2", k=8)[
                    :, :, 128 * t:128 * t + 128]
                p.emit("vector", lambda e, dst=dst: e.tensor_copy(
                    dst, PB.rearrange("p (k t2) -> p k t2", k=8)),
                    reads=(RP(PB),), writes=(r_xnT,))
            if sink is None:
                f_xn()
            else:
                sink.append(f_xn)

    def src_acc(t):
        return acc[:, t * D:(t + 1) * D], (r_acc[t],)

    def src_stage(t):
        return (big4f[:, 1024 * t:1024 * t + 1024],
                (r_big[2 * t], r_big[2 * t + 1]))

    for q in range(NQ):
        for t in range(NT):
            g = q * NT + t
            a_t = acc[:, t * D:(t + 1) * D]
            p.emit("sync",
                   lambda e, a_t=a_t, g=g: e.dma_start(out=a_t, in_=xv[g]),
                   writes=(r_acc[t],))
        if DO_ATT:
            p.emit("gpsimd", lambda e: e.dma_start(
                out=warea[:, 0:8192].rearrange("p (k j) -> p k j", k=8),
                in_=wqkv[:, 0:1024].rearrange("(k p) j -> p k j", p=128)),
                writes=(r_wblk[0], r_wblk[1]))
            p.emit("gpsimd", lambda e: e.dma_start(
                out=warea[:, 8192:16384].rearrange("p (k j) -> p k j", k=8),
                in_=wqkv[:, 1024:2048].rearrange("(k p) j -> p k j", p=128)),
                writes=(r_wblk[2], r_wblk[3]))
            p.emit("gpsimd", lambda e: e.dma_start(
                out=warea[:, 16384:24576].rearrange("p (k j) -> p k j", k=8),
                in_=wqkv[:, 2048:3072].rearrange("(k p) j -> p k j", p=128)),
                writes=(r_wblk[4], r_wblk[5]))

        if q == 0:
            inline = []
            emit_stats_router(0, src_acc, inline)
            for f in inline:
                f()
            emit_xn_tiles(0, src_acc)

        if DO_ATT:
            bank_i = 0
            for jt in range(16):
                wr = ((r_wblk[0], r_wblk[1]) if jt < 8
                      else (r_wblk[2], r_wblk[3]))
                for c in range(2):
                    bank = gu_banks[bank_i % 4]
                    bank_i += 1
                    mm_group(bank, [(
                        warea[:, (0 if jt < 8 else 8192) + 1024 * k + 128 * (jt % 8):
                              (0 if jt < 8 else 8192) + 1024 * k + 128 * (jt % 8) + 128],
                        xnT[:, TQ * k + 512 * c:TQ * k + 512 * c + 512])
                        for k in range(8)],
                        reads=wr + (r_xnT,))
                    dst = big4[:, TQ * jt + 512 * c:TQ * jt + 512 * c + 512]
                    p.emit("vector", lambda e, dst=dst, bank=bank:
                           e.tensor_copy(dst, bank),
                           reads=(RP(bank),), writes=(r_big[jt],))
            for t in range(NT):
                pr_blk = t // 2
                p0 = 64 * (t % 2)
                for dv in range(2):
                    bank = gu_banks[bank_i % 4]
                    bank_i += 1
                    base_t = 128 * t
                    mm_group(bank, [(
                        xnT[:, TQ * k + base_t:TQ * k + base_t + 64:2],
                        warea[:, 16384 + 1024 * k + 512 * dv:
                              16384 + 1024 * k + 512 * dv + 512])
                        for k in range(8)],
                        reads=(r_wblk[4], r_wblk[5], r_xnT),
                        out_ap=bank[0:32, :])
                    mm_group(bank, [(
                        xnT[:, TQ * k + base_t + 65:TQ * k + base_t + 128:2],
                        warea[:, 16384 + 1024 * k + 512 * dv:
                              16384 + 1024 * k + 512 * dv + 512])
                        for k in range(8)],
                        reads=(r_wblk[4], r_wblk[5], r_xnT),
                        out_ap=bank[32:64, :], tile_position=(0, 32))
                    dstv = v_sb[p0:p0 + 64,
                                TQ * pr_blk + 512 * dv:
                                TQ * pr_blk + 512 * dv + 512]
                    p.emit("vector",
                           lambda e, dstv=dstv, bank=bank: e.tensor_copy(
                               dstv, bank[0:64, :]),
                           reads=(RP(bank),), writes=(r_vsb[pr_blk],))

        mainline = []

        def ML(est, fn):
            mainline.append((est, fn))

        ob_rot = [PS, PD1, PD0]

        def emit_gu_item(lhs_g_fn, lhs_u_fn, rd_g, rd_u, dsth, hb_res,
                         gbank, ubank, spar):
            def fn():
                mm_group(gbank, lhs_g_fn(), reads=rd_g)
                mm_group(ubank, lhs_u_fn(), reads=rd_u)
                sil = sil_sb[:, 512 * spar:512 * spar + 512]
                p.emit("scalar", lambda e, sil=sil: e.activation(
                    sil, gbank, AF.Silu),
                    reads=(RP(gbank),), writes=(r_sil[spar],))
                p.emit("vector", lambda e, sil=sil: e.tensor_tensor(
                    dsth, sil, ubank, op=ALU.mult),
                    reads=(r_sil[spar], RP(ubank)), writes=(hb_res,))
            ML(3600, fn)

        def emit_down_item(bank, lhsrhs, reads, t, evac):
            def fn():
                mm_group(bank, lhsrhs, reads=reads)
                evac(bank)
            ML(len(lhsrhs) * 230 + 200, fn)

        if DO_SH:
            for half in range(2):
                b0 = 3 * half
                sgo, suo, sdo = 4096 * b0, 4096 * (b0 + 1), 4096 * (b0 + 2)

                def fn_dma(half=half, sgo=sgo, suo=suo, sdo=sdo, b0=b0):
                    p.emit("gpsimd", lambda e: e.dma_start(
                        out=warea[:, sgo:sgo + 4096].rearrange(
                            "p (k j) -> p k j", k=8),
                        in_=shgT[:, 512 * half:512 * half + 512].rearrange(
                            "(k p) j -> p k j", p=128)),
                        writes=(r_wblk[b0],))
                    p.emit("gpsimd", lambda e: e.dma_start(
                        out=warea[:, suo:suo + 4096].rearrange(
                            "p (k j) -> p k j", k=8),
                        in_=shuT[:, 512 * half:512 * half + 512].rearrange(
                            "(k p) j -> p k j", p=128)),
                        writes=(r_wblk[b0 + 1],))
                    p.emit("gpsimd", lambda e: e.dma_start(
                        out=warea[:, sdo:sdo + 4096].rearrange(
                            "p (k j) -> p k j", k=4),
                        in_=shdT[512 * half:512 * half + 512, :].rearrange(
                            "(k p) j -> p k j", p=128)),
                        writes=(r_wblk[b0 + 2],))
                ML(0, fn_dma)
                gu_i = 0
                for icc in range(4):
                    for c in range(2):
                        gbank = gu_banks[gu_i % 4]
                        ubank = gu_banks[(gu_i + 1) % 4]
                        gu_i += 2
                        spar = (2 * icc + c) % 2
                        dsth = hbuf[:, 1024 * icc + 512 * c:
                                    1024 * icc + 512 * c + 512]
                        emit_gu_item(
                            lambda icc=icc, c=c, sgo=sgo: [(
                                warea[:, sgo + 512 * k + 128 * icc:
                                      sgo + 512 * k + 128 * icc + 128],
                                xnT[:, TQ * k + 512 * c:TQ * k + 512 * c + 512])
                                for k in range(8)],
                            lambda icc=icc, c=c, suo=suo: [(
                                warea[:, suo + 512 * k + 128 * icc:
                                      suo + 512 * k + 128 * icc + 128],
                                xnT[:, TQ * k + 512 * c:TQ * k + 512 * c + 512])
                                for k in range(8)],
                            (r_wblk[b0], r_xnT), (r_wblk[b0 + 1], r_xnT),
                            dsth, r_hb[icc], gbank, ubank, spar)
                for t in range(NT):
                    for dd in range(2):
                        bank = ob_rot[dd % 2]

                        def evac(bank, t=t, dd=dd):
                            a_sl = acc[:, D * t + 512 * dd:
                                       D * t + 512 * dd + 512]
                            p.emit("vector", lambda e, a_sl=a_sl, bank=bank:
                                   e.tensor_tensor(a_sl, a_sl, bank,
                                                   op=ALU.add),
                                   reads=(RP(bank),), writes=(r_acc[t],))
                        emit_down_item(
                            bank,
                            [(hbuf[:, 1024 * icc2 + 128 * t:
                                   1024 * icc2 + 128 * t + 128],
                              warea[:, sdo + 1024 * icc2 + 512 * dd:
                                    sdo + 1024 * icc2 + 512 * dd + 512])
                             for icc2 in range(4)],
                            (r_hb[0], r_hb[1], r_hb[2], r_hb[3],
                             r_wblk[b0 + 2]),
                            t, evac)

        if DO_MOE:
            def emit_exp_dma_ml(ei):
                slot = ei % 2
                sofs = 12288 * slot
                sblks = (r_wblk[3 * slot], r_wblk[3 * slot + 1],
                         r_wblk[3 * slot + 2])

                def fn(sofs=sofs, ei=ei, sblks=sblks):
                    p.emit("gpsimd", lambda e: e.dma_start(
                        out=warea[:, sofs:sofs + 4096].rearrange(
                            "p (k i) -> p k i", k=8),
                        in_=gwT[ei].rearrange("(k p) i -> p k i", p=128)),
                        writes=(sblks[0],))
                    p.emit("gpsimd", lambda e: e.dma_start(
                        out=warea[:, sofs + 4096:sofs + 8192].rearrange(
                            "p (k i) -> p k i", k=8),
                        in_=uwT[ei].rearrange("(k p) i -> p k i", p=128)),
                        writes=(sblks[1],))
                    p.emit("gpsimd", lambda e: e.dma_start(
                        out=warea[:, sofs + 8192:sofs + 12288].rearrange(
                            "p (k h) -> p k h", k=4),
                        in_=dwT[ei].rearrange("(k p) h -> p k h", p=128)),
                        writes=(sblks[2],))
                ML(0, fn)

            emit_exp_dma_ml(0)
            emit_exp_dma_ml(1)
            cbv = comb[:, 128 * (q % 2):128 * (q % 2) + 128].rearrange(
                "p (e2 t) -> p e2 t", t=NT)
            for ei in range(E):
                slot = ei % 2
                sofs = 12288 * slot
                sblks = (r_wblk[3 * slot], r_wblk[3 * slot + 1],
                         r_wblk[3 * slot + 2])
                gu_i = 0
                for ic in range(4):
                    for c in range(2):
                        gbank = gu_banks[gu_i % 4]
                        ubank = gu_banks[(gu_i + 1) % 4]
                        gu_i += 2
                        spar = (2 * ic + c) % 2
                        dsth = hbuf[:, 1024 * ic + 512 * c:
                                    1024 * ic + 512 * c + 512]
                        emit_gu_item(
                            lambda ic=ic, c=c, sofs=sofs: [(
                                warea[:, sofs + 512 * k + 128 * ic:
                                      sofs + 512 * k + 128 * ic + 128],
                                xnT[:, TQ * k + 512 * c:TQ * k + 512 * c + 512])
                                for k in range(8)],
                            lambda ic=ic, c=c, sofs=sofs: [(
                                warea[:, sofs + 4096 + 512 * k + 128 * ic:
                                      sofs + 4096 + 512 * k + 128 * ic + 128],
                                xnT[:, TQ * k + 512 * c:TQ * k + 512 * c + 512])
                                for k in range(8)],
                            (sblks[0], r_xnT), (sblks[1], r_xnT),
                            dsth, r_hb[ic], gbank, ubank, spar)
                rot = ob_rot if ei < 4 else [PS, PD1, PAV, PD0]
                for t in range(NT):
                    for dd in range(2):
                        bank = rot[(2 * t + dd) % len(rot)]

                        def evac(bank, t=t, dd=dd, ei=ei, cbv=cbv, q=q):
                            a_sl = acc[:, D * t + 512 * dd:
                                       D * t + 512 * dd + 512]
                            p.emit("vector",
                                   lambda e, a_sl=a_sl, bank=bank, ei=ei,
                                   t=t, cbv=cbv:
                                   e.scalar_tensor_tensor(
                                       a_sl, bank, cbv[:, ei, t:t + 1], a_sl,
                                       op0=ALU.mult, op1=ALU.add),
                                   reads=(RP(bank), r_comb2[q % 2]),
                                   writes=(r_acc[t],))
                        emit_down_item(
                            bank,
                            [(hbuf[:, 1024 * ic2 + 128 * t:
                                   1024 * ic2 + 128 * t + 128],
                              warea[:, sofs + 8192 + 1024 * ic2 + 512 * dd:
                                    sofs + 8192 + 1024 * ic2 + 512 * dd + 512])
                             for ic2 in range(4)],
                            (r_hb[0], r_hb[1], r_hb[2], r_hb[3], sblks[2]),
                            t, evac)
                if ei + 2 < E:
                    emit_exp_dma_ml(ei + 2)

        att_steps = []

        if DO_ATT and ATT_LVL >= 3:
            for h in range(NH):
                hp = h // 2
                hpar = h % 2
                spar = h % 2
                qjt = hp
                kjt = 8 + hp
                q_r0 = 64 * hpar

                def fn_scores(h=h, hp=hp, hpar=hpar, spar=spar, qjt=qjt,
                              kjt=kjt, q_r0=q_r0):
                    for n in range(NT):
                        lhs = big4[q_r0:q_r0 + 64,
                                   TQ * qjt + 128 * n:TQ * qjt + 128 * n + 128]
                        kbase = TQ * kjt + 128 * n
                        rhs_a = big4[q_r0:q_r0 + 64, kbase:kbase + 64:2]
                        rhs_b = big4[q_r0:q_r0 + 64,
                                     kbase + 65:kbase + 128:2]
                        p.emit("tensor",
                               lambda e, lhs=lhs, rhs_a=rhs_a, n=n: e.matmul(
                                   PR[:, 64 * n:64 * n + 32], lhs, rhs_a,
                                   start=True, stop=True),
                               reads=(r_big[qjt], r_big[kjt]),
                               writes=(RP(PR),))
                        p.emit("tensor",
                               lambda e, lhs=lhs, rhs_b=rhs_b, n=n: e.matmul(
                                   PR[:, 64 * n + 32:64 * n + 64], lhs,
                                   rhs_b, start=True, stop=True),
                               reads=(r_big[qjt], r_big[kjt]),
                               writes=(RP(PR),))
                    smb = sm8[:, 32 * spar:32 * spar + 32]
                    m1, nb1 = smb[:, 0:1], smb[:, 1:2]
                    s8, r8 = smb[:, 16:24], smb[:, 24:32]
                    p.emit("vector", lambda e, m1=m1: e.reduce_max(
                        m1, PR, axis=AX.X),
                        reads=(RP(PR),), writes=(r_sm8[spar],))
                    p.emit("vector", lambda e, m1=m1, nb1=nb1: e.tensor_scalar(
                        nb1, m1, -0.125, None, op0=ALU.mult),
                        reads=(r_sm8[spar],), writes=(r_sm8[spar],))
                    expd = exp_sb[:, 512 * spar:512 * spar + 512]
                    p.emit("scalar", lambda e, expd=expd, nb1=nb1:
                           e.activation(expd, PR, AF.Exp,
                                        bias=nb1, scale=0.125),
                           reads=(RP(PR), r_sm8[spar]),
                           writes=(r_exps[spar],))
                    e3 = expd.rearrange("p (n k) -> p n k", n=8)
                    p.emit("vector", lambda e, e3=e3, s8=s8: e.reduce_sum(
                        s8, e3, axis=AX.X),
                        reads=(r_exps[spar],), writes=(r_sm8[spar],))
                    p.emit("vector", lambda e, s8=s8, r8=r8: e.reciprocal(
                        r8, s8),
                        reads=(r_sm8[spar],), writes=(r_sm8[spar],))
                    attnd = attn_sb[:, 512 * spar:512 * spar + 512]
                    for n in range(NT):
                        p.emit("vector",
                               lambda e, n=n, attnd=attnd, expd=expd, r8=r8:
                               e.tensor_scalar(
                                   attnd[:, 64 * n:64 * n + 64],
                                   expd[:, 64 * n:64 * n + 64],
                                   r8[:, n:n + 1], None, op0=ALU.mult),
                               reads=(r_exps[spar], r_sm8[spar]),
                               writes=(r_attns[spar],))
                att_steps.append((12000, fn_scores))

                if ATT_LVL >= 4:
                    def fn_tr(h=h, spar=spar):
                        attnd = attn_sb[:, 512 * spar:512 * spar + 512]
                        for pr2 in range(4):
                            p.emit("tensor",
                                   lambda e, pr2=pr2, attnd=attnd:
                                   e.transpose(
                                       PB[:, 128 * pr2:128 * pr2 + 128],
                                       attnd[:, 128 * pr2:128 * pr2 + 128],
                                       identb),
                                   reads=(r_attns[spar], r_const),
                                   writes=(RP(PB),))
                        attnTd = attnT_sb[:, 512 * spar:512 * spar + 512]
                        p.emit("vector", lambda e, attnTd=attnTd:
                               e.tensor_copy(attnTd, PB[:, 0:512]),
                               reads=(RP(PB),), writes=(r_attnT[spar],))
                    att_steps.append((2500, fn_tr))

                if ATT_LVL >= 5:
                    def fn_av(h=h, hp=hp, hpar=hpar, spar=spar):
                        attnTd = attnT_sb[:, 512 * spar:512 * spar + 512]
                        for n in range(NT):
                            prb = n // 2
                            vr0 = 64 * (n % 2)
                            av_bank = PR if n % 2 == 0 else PAV
                            lhsv = v_sb[vr0:vr0 + 64,
                                        TQ * prb + 64 * h:
                                        TQ * prb + 64 * h + 64]
                            rhsa = attnTd[vr0:vr0 + 64,
                                          128 * prb:128 * prb + 128]
                            p.emit("tensor",
                                   lambda e, lhsv=lhsv, rhsa=rhsa, n=n,
                                   hpar=hpar, av_bank=av_bank: e.matmul(
                                       av_bank[64 * hpar:64 * hpar + 64,
                                               128 * (n // 2):
                                               128 * (n // 2) + 128],
                                       lhsv, rhsa, start=True, stop=True,
                                       tile_position=(64 * (n % 2),
                                                      64 * hpar)),
                                   reads=(r_vsb[prb], r_attnT[spar]),
                                   writes=(RP(av_bank),))
                        r0 = 64 * hpar
                        for par2 in range(2):
                            av_bank = PR if par2 == 0 else PAV
                            base = TQ * hp + 128 * par2
                            dsta = _four_block_cols(aoT, base, 256, 128)
                            p.emit("vector",
                                   lambda e, dsta=dsta, av_bank=av_bank,
                                   r0=r0: e.tensor_copy(
                                       dsta[r0:r0 + 64],
                                       av_bank[r0:r0 + 64, 0:512]),
                                   reads=(RP(av_bank),),
                                   writes=(r_aoT[hp],))
                    att_steps.append((1500, fn_av))

        pa_items = []
        if q + 1 < NQ:
            for t in range(NT):
                g2 = (q + 1) * NT + t

                def f_stage(t=t, g2=g2):
                    p.emit("sync", lambda e, t=t, g2=g2: e.dma_start(
                        out=big4f[:, 1024 * t:1024 * t + 1024], in_=xv[g2]),
                        writes=(r_big[2 * t], r_big[2 * t + 1]))
                pa_items.append(f_stage)
            emit_stats_router(q + 1, src_stage, pa_items)

        ml_pos = [0]

        def pump(ns):
            acc_ns = 0
            while ml_pos[0] < len(mainline) and acc_ns < ns:
                est, fn = mainline[ml_pos[0]]
                ml_pos[0] += 1
                fn()
                acc_ns += est

        for i, (fill, fn) in enumerate(att_steps):
            fn()
            pump(fill if i >= 4 else 2000)
        for f in pa_items:
            f()
            pump(6000)
        while ml_pos[0] < len(mainline):
            mainline[ml_pos[0]][1]()
            ml_pos[0] += 1

        ay_items = []
        if DO_ATT and ATT_LVL >= 6:
            p.emit("gpsimd", lambda e: e.dma_start(
                out=warea[:, 8192:16384].rearrange("p (k j) -> p k j", k=8),
                in_=wout.rearrange("(k p) j -> p k j", p=128)),
                writes=(r_wblk[2], r_wblk[3]))
            for t in range(NT):
                for dd in range(2):
                    bank = ob_rot[dd % 2]

                    def f_ay(t=t, dd=dd, bank=bank):
                        mm_group(bank, [(
                            aoT[:, TQ * jt2 + 128 * t:
                                TQ * jt2 + 128 * t + 128],
                            warea[:, 8192 + 1024 * jt2 + 512 * dd:
                                  8192 + 1024 * jt2 + 512 * dd + 512])
                            for jt2 in range(8)],
                            reads=(r_aoT[0], r_aoT[1], r_aoT[2], r_aoT[3],
                                   r_aoT[4], r_aoT[5], r_aoT[6], r_aoT[7],
                                   r_wblk[2], r_wblk[3]))
                        a_sl = acc[:, D * t + 512 * dd:
                                   D * t + 512 * dd + 512]
                        p.emit("vector", lambda e, a_sl=a_sl, bank=bank:
                               e.tensor_tensor(a_sl, a_sl, bank, op=ALU.add),
                               reads=(RP(bank),), writes=(r_acc[t],))
                    ay_items.append(f_ay)
        xn_items = []
        if q + 1 < NQ:
            emit_xn_tiles(q + 1, src_stage, sink=xn_items)
        tails = [ay_items, xn_items]
        while any(tails):
            for tl in tails:
                if tl:
                    tl.pop(0)()

        p.emit("sync", lambda e, q=q: e.dma_start(
            out=outv[q], in_=acc.rearrange("p (t d) -> p t d", t=NT)),
            reads=tuple(r_acc), writes=(r_out,))

    p.emit("vector", lambda e: e.nop(), reads=(r_out,))

    import contextlib
    with contextlib.ExitStack() as stack:
        sems = {}
        for e in ENGINES:
            sems[e] = stack.enter_context(nc.semaphore(name=f"s_{e}"))
        block = stack.enter_context(nc.Block())
        p.finalize(block, sems)
    return nc


def _two_block_cols(buf, base, stride, width):
    v = buf[:, base:base + stride + 1:stride]
    w = buf[:, base:base + width]
    u = v.copy()
    u.ap = v.ap + w.ap[1:]
    return u


def _four_block_cols(buf, base, stride, width):
    v = buf[:, base:base + 3 * stride + 1:stride]
    w = buf[:, base:base + width]
    u = v.copy()
    u.ap = v.ap + w.ap[1:]
    return u


def _ksub_cols(buf, base):
    v = buf[:, base:base + 66:65]
    w = buf[:, base:base + 64:2]
    u = v.copy()
    u.ap = v.ap + w.ap[1:]
    return u


def _ksub_ap(big4, r0, kbase):
    v = big4[r0:r0 + 64, kbase:kbase + 66:65]
    w = big4[r0:r0 + 64, kbase:kbase + 64:2]
    u = v.copy()
    u.ap = v.ap + w.ap[1:]
    return u


_NC_CACHE = None


def kernel(L0_attn_x, L0_attn_W_qkv, L0_attn_W_out, L0_ffn_router_weight,
           L0_ffn_e_bias, L0_ffn_gate_w, L0_ffn_up_w, L0_ffn_down_w,
           L0_ffn_sh_gate_w, L0_ffn_sh_up_w, L0_ffn_sh_down_w):
    global _NC_CACHE
    f32 = np.float32

    def A(v):
        return np.ascontiguousarray(np.asarray(v, dtype=f32))

    x = A(L0_attn_x)
    wqkv = A(L0_attn_W_qkv)
    wout = A(L0_attn_W_out)
    rwT = A(np.asarray(L0_ffn_router_weight, f32).T)
    eb = np.asarray(L0_ffn_e_bias, f32)
    ebias_rep = A(np.repeat(eb, NT)[None, :])
    eps16_rep = A(np.repeat((15 - np.arange(E)) * 1e-30, NT)[None, :])
    epsg_rep = A(np.repeat((3 - np.arange(4)) * 1e-30, NT)[None, :])
    gwT = A(np.asarray(L0_ffn_gate_w, f32).transpose(0, 2, 1))
    uwT = A(np.asarray(L0_ffn_up_w, f32).transpose(0, 2, 1))
    dwT = A(np.asarray(L0_ffn_down_w, f32).transpose(0, 2, 1))
    shgT = A(np.asarray(L0_ffn_sh_gate_w, f32).T)
    shuT = A(np.asarray(L0_ffn_sh_up_w, f32).T)
    shdT = A(np.asarray(L0_ffn_sh_down_w, f32).T)
    identf = np.eye(128, dtype=f32)

    if _NC_CACHE is None:
        _NC_CACHE = build_program()
    nc = _NC_CACHE

    in_maps = []
    for c in range(N_CORES):
        xc = np.ascontiguousarray(x[c])
        xTt = np.ascontiguousarray(
            xc.reshape(T // 128, 128, D).transpose(0, 2, 1))
        in_maps.append({
            "x": xc, "xTt": xTt, "wqkv": wqkv, "wout": wout, "rwT": rwT,
            "ebias_rep": ebias_rep, "eps16_rep": eps16_rep,
            "epsg_rep": epsg_rep, "gwT": gwT, "uwT": uwT, "dwT": dwT,
            "shgT": shgT, "shuT": shuT, "shdT": shdT, "identf": identf,
        })
    res = run_bass_kernel_spmd(nc, in_maps, core_ids=list(range(N_CORES)))
    return np.stack([r["out"] for r in res.results], axis=0)
